# version 03
# baseline (speedup 1.0000x reference)
import contextlib
import math
import numpy as np
import ml_dtypes
import concourse.bass as bass
import concourse.mybir as mybir
from concourse.bass_utils import run_bass_kernel_spmd

F32 = mybir.dt.float32
BF16 = mybir.dt.bfloat16
AF = mybir.ActivationFunctionType
ALU = mybir.AluOpType
BF = ml_dtypes.bfloat16

NB, SEQ, D, NH, HD, INTER = 2, 2048, 2048, 16, 128, 8192
NSC, SCT, NTT = 8, 512, 4
L2_EPS, RMS_EPS, LN_EPS = 1e-6, 1e-5, 1e-5
NCORES = 8


def _sylvester(n):
    H = np.array([[1.0]], dtype=np.float32)
    while H.shape[0] < n:
        H = np.block([[H, H], [H, -H]])
    return H.astype(np.float32)


def _host_consts():
    c = {}
    c["h128"] = _sylvester(128)
    h16 = _sylvester(16)
    bd16 = np.zeros((128, 128), dtype=np.float32)
    for j in range(8):
        bd16[16 * j:16 * j + 16, 16 * j:16 * j + 16] = h16
    c["bd16"] = bd16
    h64 = _sylvester(64)
    bd64 = np.zeros((128, 128), dtype=np.float32)
    for j in range(2):
        bd64[64 * j:64 * j + 64, 64 * j:64 * j + 64] = h64
    c["bd64"] = bd64
    c["i128"] = np.eye(128, dtype=np.float32)
    c["mstneg"] = np.tile(np.tril(np.ones((128, 128), np.float32), -1) * -1.0,
                          (1, 4))
    c["mtriu"] = np.tile(np.triu(np.ones((128, 128), np.float32), 0), (1, 4))
    esum = np.zeros((128, 256), dtype=np.float32)
    for h in range(16):
        esum[:, 16 * h + h] = 1.0
    c["esum"] = esum
    ebc = np.zeros((16, 2048), dtype=np.float32)
    for h in range(16):
        ebc[h, 128 * h:128 * h + 128] = 1.0
    c["ebc"] = ebc
    c["ones1"] = np.ones((1, 128), dtype=np.float32)
    return {k: v.astype(BF) for k, v in c.items()}


class _View:
    __slots__ = ("lo", "hi", "name", "w", "r", "overlaps", "space")

    def __init__(self, lo, hi, name, space):
        self.lo, self.hi = lo, hi
        self.name, self.space = name, space
        self.w = None
        self.r = {}
        self.overlaps = []


class _KB:
    ENG_STREAM = {"tensor": "pe", "vector": "dve", "scalar": "act",
                  "gpsimd": "pool", "sync": "spq"}
    DMA_STREAM = {"sync": "spq", "scalar": "actq", "gpsimd": "poolq"}

    QSUBS = 6

    def __init__(self, nc):
        self.nc = nc
        self.ops = {e: [] for e in ("tensor", "vector", "scalar", "gpsimd",
                                    "sync")}
        self.counts = {s: 0 for s in ("pe", "dve", "act", "pool")}
        for q in ("spq", "actq", "poolq"):
            for i in range(self.QSUBS):
                self.counts[q + str(i)] = 0
        self.qidx = {"spq": 0, "actq": 0, "poolq": 0}
        self.wm = {s: {} for s in self.counts}
        self.views = []

    def view(self, lo, hi, name, space):
        v = _View(lo, hi, name, space)
        for o in self.views:
            if o.space == space and not (hi <= o.lo or lo >= o.hi):
                o.overlaps.append(v)
                v.overlaps.append(o)
        self.views.append(v)
        return v

    paranoid = False

    def rec(self, eng, fn, reads=(), writes=(), dma=False, inc=True):
        if dma:
            qbase = self.DMA_STREAM[eng]
            stream = qbase + str(self.qidx[qbase] % self.QSUBS)
            self.qidx[qbase] += 1
        else:
            stream = self.ENG_STREAM[eng]
        step = 16 if dma else 1
        waits = {}
        wm = self.wm[stream]
        cur = self.counts[stream]

        def need(s, c):
            if s == stream and (stream == "pe" or c > cur):
                return
            if wm.get(s, 0) < c and waits.get(s, 0) < c:
                waits[s] = c

        if dma and cur > 0:
            need(stream, cur)

        for u in reads:
            for v in (u, *u.overlaps):
                if v.w:
                    need(*v.w)
        for u in writes:
            for v in (u, *u.overlaps):
                if v.w:
                    need(*v.w)
                for s, c in v.r.items():
                    need(s, c)
        if inc:
            cnt = self.counts[stream] + step
            self.counts[stream] = cnt
        else:
            assert stream == "pe"
            cnt = self.counts[stream] + 1
        for s, c in waits.items():
            wm[s] = c
        for u in reads:
            if u.r.get(stream, 0) < cnt:
                u.r[stream] = cnt
        for u in writes:
            u.w = (stream, cnt)
            u.r = {}
        self.ops[eng].append((waits, fn, stream if inc else None, step))


class _Buf:
    __slots__ = ("t", "col0", "ncol", "nrow", "row0", "view")

    def __init__(self, kb, tensor, col0, ncol, name, space, itemsz,
                 nrow=128, row0=0):
        self.t, self.col0, self.ncol = tensor, col0, ncol
        self.nrow, self.row0 = nrow, row0
        self.view = kb.view(col0 * itemsz, (col0 + ncol) * itemsz, name, space)

    def __getitem__(self, sl):
        if isinstance(sl, tuple):
            rows, cols = sl
        else:
            rows, cols = slice(None), sl
        r0 = self.row0 if rows.start is None else self.row0 + rows.start
        r1 = self.row0 + self.nrow if rows.stop is None else self.row0 + rows.stop
        c0 = self.col0 if cols.start is None else self.col0 + cols.start
        c1 = self.col0 + self.ncol if cols.stop is None else self.col0 + cols.stop
        return self.t[r0:r1, c0:c1]

    @property
    def ap(self):
        return self.t[self.row0:self.row0 + self.nrow,
                      self.col0:self.col0 + self.ncol]


def build_program(cfg=None):
    cfg = cfg or {}
    n_sc = cfg.get("n_sc", NSC)
    dumps = cfg.get("dumps", [])
    skip_ffn = cfg.get("skip_ffn", False)
    attn_stage = cfg.get("attn_stage", 99)
    dlim = cfg.get("delta_limit", (4, 4))
    nc = bass.Bass("TRN2", target_bir_lowering=False)
    kb = _KB(nc)

    def din(name, shape, dt=F32):
        return nc.dram_tensor(name, list(shape), dt, kind="ExternalInput").ap()

    x_tm_d = din("x_tm", (NB, SEQ, D))
    x_fm_d = din("x_fm", (NB, D, SEQ))
    wq_td = din("wq_t", (16, 128, 2048))
    wk_td = din("wk_t", (16, 128, 2048))
    wv_td = din("wv_t", (16, 128, 2048))
    wb_td = din("wb_t", (D, NH))
    wo_td = din("wo_t", (4, 16, 128, 512))
    conv_d = din("conv_w", (3, 128, 64))
    rms_d = din("rms_w", (128, 1))
    wup_d = din("wup_t", (D, INTER))
    wg_d = din("wg_t", (D, INTER))
    wd_d = din("wd_t", (INTER, D))
    suug_d = din("su_ug", (128, 16))
    sudp_d = din("su_dp", (128, 64))
    wsup_d = din("ws_up", (128, 64))
    wsgt_d = din("ws_gt", (128, 64))
    wsdp_d = din("ws_dp", (1, 2048))
    consts_np = _host_consts()
    cst_d = {k: din("c_" + k, v.shape, BF16) for k, v in consts_np.items()}
    out_d = nc.dram_tensor("out", [NB, SEQ, D], F32, kind="ExternalOutput").ap()
    trace = cfg.get("trace", False)
    if trace:
        trq_d = nc.dram_tensor("tr_q", [NSC, 128, 8256], BF16, kind="ExternalOutput").ap()
        trk_d = nc.dram_tensor("tr_k", [NSC, 128, 8256], BF16, kind="ExternalOutput").ap()
        trv_d = nc.dram_tensor("tr_v", [NSC, 128, 8256], BF16, kind="ExternalOutput").ap()
        tron_d = nc.dram_tensor("tr_on", [NSC, 128, 8192], BF16, kind="ExternalOutput").ap()
        trx1_d = nc.dram_tensor("tr_x1", [NSC, 128, 8192], BF16, kind="ExternalOutput").ap()
        trb_d = nc.dram_tensor("tr_b", [NSC, 128, 64], F32, kind="ExternalOutput").ap()
        trxf_d = nc.dram_tensor("tr_x1fm", [NSC, 128, 8192], BF16, kind="ExternalOutput").ap()
        try_d = nc.dram_tensor("tr_y", [NSC, 128, 16384], BF16, kind="ExternalOutput").ap()
        trh_d = nc.dram_tensor("tr_h2", [NSC, 128, 4096], F32, kind="ExternalOutput").ap()

    expose = cfg.get("expose", ())

    def dscr(name, shape, dt=BF16):
        kind = "ExternalOutput" if name in expose else "Internal"
        return nc.dram_tensor(name, list(shape), dt, kind=kind).ap()

    wq_s = dscr("wq_s", (16, 128, 2048))
    wk_s = dscr("wk_s", (16, 128, 2048))
    wv_s = dscr("wv_s", (16, 128, 2048))
    wo_s = dscr("wo_s", (4, 16, 128, 512))
    dq_s = dscr("dq_s", (3, 128, 8192))
    upA_s = dscr("upA_s", (D, INTER))
    gtA_s = dscr("gtA_s", (D, INTER))
    dnA_s = dscr("dnA_s", (INTER, D))
    up_s = dscr("up_s", (64, 128, 2048))
    gt_s = dscr("gt_s", (64, 128, 2048))
    dn_s = dscr("dn_s", (16, 128, 8192))

    ctx = contextlib.ExitStack()
    sb = {}

    def alloc(name, shape, dt):
        t = ctx.enter_context(nc.sbuf_tensor(name, list(shape), dt))
        sb[name] = t
        return t

    for k, v in consts_np.items():
        alloc("k_" + k, v.shape, BF16)
    alloc("ccol", (128, 192), F32)
    alloc("rmsc", (128, 1), F32)
    alloc("sup", (128, 64), F32)
    alloc("sgt", (128, 64), F32)
    alloc("sucu", (128, 16), F32)
    alloc("sucd", (128, 64), F32)
    alloc("wbb", (128, 256), BF16)
    alloc("tails", (128, 192), BF16)
    alloc("st32", (128, 4096), F32)
    alloc("stbf", (128, 4096), BF16)
    alloc("beta", (128, 64), F32)
    PBF_COLS = 61888
    PF32_COLS = 8300
    alloc("pbf", (128, PBF_COLS), BF16)
    alloc("pf32", (128, PF32_COLS), F32)
    pbanks = [ctx.enter_context(nc.psum_tensor("ps%d" % i, [128, 512], F32))
              for i in range(6)]
    ptb = [ctx.enter_context(nc.psum_tensor("pt%d" % i, [128, 1024], BF16))
           for i in range(2)]

    B = {}
    for k, v in consts_np.items():
        B[k] = _Buf(kb, sb["k_" + k], 0, v.shape[1], k, "t_" + k, 2,
                    nrow=v.shape[0])
    for nm, isz in (("ccol", 4), ("rmsc", 4), ("sup", 4), ("sgt", 4),
                    ("sucu", 4), ("sucd", 4), ("wbb", 2), ("tails", 2),
                    ("beta", 4)):
        t = sb[nm]
        B[nm] = _Buf(kb, t, 0, t.shape[1], nm, "t_" + nm, isz, nrow=t.shape[0])

    ST32 = [_Buf(kb, sb["st32"], bh * 128, 128, "st32_%d" % bh, "t_st32", 4)
            for bh in range(32)]
    STBF = [_Buf(kb, sb["stbf"], bh * 128, 128, "stbf_%d" % bh, "t_stbf", 2)
            for bh in range(32)]
    STG32 = [_Buf(kb, sb["st32"], g * 512, 512, "stg32_%d" % g, "t_st32", 4)
             for g in range(8)]
    STGBF = [_Buf(kb, sb["stbf"], g * 512, 512, "stgbf_%d" % g, "t_stbf", 2)
             for g in range(8)]
    STALL32 = _Buf(kb, sb["st32"], 0, 4096, "st32_all", "t_st32", 4)
    STALLBF = _Buf(kb, sb["stbf"], 0, 4096, "stbf_all", "t_stbf", 2)

    class _PBank:
        def __init__(self, t, view):
            self.t, self.view = t, view

        def __getitem__(self, sl):
            return self.t[sl]

    PS = [_PBank(pbanks[i], kb.view(0, 1, "ps%d" % i, "zps%d" % i))
          for i in range(6)]
    PT = [_PBank(ptb[i], kb.view(0, 1, "pt%d" % i, "zpt%d" % i))
          for i in range(2)]

    dvw = {}
    for nm in ("wq_s", "wk_s", "wv_s", "wo_s", "dq_s", "upA_s", "gtA_s",
               "dnA_s", "up_s", "gt_s", "dn_s", "out"):
        dvw[nm] = kb.view(0, 1, nm, "dram_" + nm)

    def dma(q, out_ap, in_ap, reads=(), writes=()):
        kb.rec(q, lambda e, o=out_ap, i=in_ap: e.dma_start(out=o, in_=i),
               reads=reads, writes=writes, dma=True)

    def mm(out_ap, lhsT_ap, rhs_ap, reads, wbank, start, stop, inc):
        kb.rec("tensor",
               lambda e, o=out_ap, l=lhsT_ap, r=rhs_ap, s1=start, s2=stop:
               e.matmul(o, l, r, start=s1, stop=s2),
               reads=reads, writes=(wbank.view,), inc=inc)

    def tpose(out_ap, in_ap, reads, wbank, inc):
        kb.rec("tensor",
               lambda e, o=out_ap, i=in_ap: e.transpose(o, i, B["i128"].ap),
               reads=tuple(reads) + (B["i128"].view,), writes=(wbank.view,),
               inc=inc)

    def vop(fn, reads, writes):
        kb.rec("vector", fn, reads=reads, writes=writes)

    def sop(fn, reads, writes):
        kb.rec("scalar", fn, reads=reads, writes=writes)

    def bfb(col0, ncol, name, nrow=128, row0=0):
        assert col0 + ncol <= PBF_COLS, (name, col0 + ncol)
        return _Buf(kb, sb["pbf"], col0, ncol, name, "bf", 2, nrow, row0)

    def f32b(col0, ncol, name, nrow=128, row0=0):
        assert col0 + ncol <= PF32_COLS, (name, col0 + ncol)
        return _Buf(kb, sb["pf32"], col0, ncol, name, "f32", 4, nrow, row0)

    XFM = bfb(0, 8192, "xfm")
    RAWQ = bfb(8192, 8256, "rawq")
    RAWK = bfb(16448, 8256, "rawk")
    RAWV = bfb(24704, 8256, "rawv")
    KNTM = bfb(32960, 2048, "kn_tm")
    VTM = bfb(35008, 2048, "v_tm")
    XB = bfb(37056, 512, "xb")
    ANEG = bfb(37568, 512, "aneg")
    BTS = bfb(38080, 3584, "bts")
    BPP = bfb(41664, 1024, "b_pp")
    MTB = bfb(42688, 512, "mt")
    SQT = bfb(43200, 1024, "sqtmp")
    RSBF = bfb(44224, 1024, "rsbf", nrow=16)
    ONFM = bfb(45248, 8192, "onfm")
    WRING = bfb(53440, 4096, "wring")
    WORING = bfb(53440, 8192, "woring")
    DQ = bfb(0, 8192, "dq")
    YB = bfb(8192, 16384, "y")
    X1B = bfb(24704, 8192, "x1b")
    X1FM = bfb(0, 8192, "x1fm")
    UPR = bfb(32960, 4096, "upring")
    GTR = bfb(37056, 4096, "gatering")
    WDR = bfb(45248, 16384, "wdring")
    SILU = bfb(43200, 256, "silu_tmp")
    SQD = bfb(8192, 2048, "sqdump")
    SDPR = bfb(41664, 2048, "sdprep")
    SAI = bfb(0, 8192, "stgA_in")
    SAO = bfb(8192, 8192, "stgA_out")
    WOPR = bfb(16448, 4096, "wo_prep")
    DGPR = bfb(20544, 512, "diag_prep")
    XM = f32b(0, 512, "xm")
    OTMP = f32b(512, 512, "o_tmp")
    OUTB = f32b(1024, 4096, "outbuf")
    OUTT = [_Buf(kb, sb["pf32"], 1024 + ttl * 2048, 2048, "outb%d" % ttl,
                 "f32", 4) for ttl in range(2)]
    RS32 = f32b(5120, 1024, "rs32", nrow=16)
    SUC = f32b(6144, 1, "su_col")
    STATS = f32b(6145, 16, "stats")
    RMSV = f32b(6161, 16, "rmsv")

    for nm in consts_np:
        dma("sync", B[nm].ap, cst_d[nm], writes=(B[nm].view,))
    dma("sync", B["rmsc"].ap, rms_d, writes=(B["rmsc"].view,))
    for pi in range(3):
        dma("sync", B["ccol"][:, pi * 64:(pi + 1) * 64], conv_d[pi],
            writes=(B["ccol"].view,))
    dma("gpsimd", B["wbb"].ap, wb_td.rearrange("(i p) h -> p i h", p=128),
        writes=(B["wbb"].view,))
    dma("sync", B["sup"].ap, wsup_d, writes=(B["sup"].view,))
    dma("sync", B["sgt"].ap, wsgt_d, writes=(B["sgt"].view,))
    c2048 = 1.0 / math.sqrt(2048.0)
    c8192 = 1.0 / math.sqrt(8192.0)
    vop(lambda e: e.tensor_scalar(out=B["sup"].ap, in0=B["sup"].ap,
                                  scalar1=c2048, scalar2=None, op0=ALU.mult),
        (B["sup"].view,), (B["sup"].view,))
    vop(lambda e: e.tensor_scalar(out=B["sgt"].ap, in0=B["sgt"].ap,
                                  scalar1=c2048, scalar2=None, op0=ALU.mult),
        (B["sgt"].view,), (B["sgt"].view,))
    for j in range(16):
        dma("sync", B["sucu"][:, j:j + 1], suug_d[8 * j:8 * j + 8, :],
            writes=(B["sucu"].view,))
    for j in range(64):
        dma("sync", B["sucd"][:, j:j + 1], sudp_d[2 * j:2 * j + 2, :],
            writes=(B["sucd"].view,))
    dma("gpsimd", SDPR[0:1, 0:2048], wsdp_d, writes=(SDPR.view,))
    vop(lambda e: e.tensor_scalar(out=SDPR[0:1, 0:2048], in0=SDPR[0:1, 0:2048],
                                  scalar1=c8192, scalar2=None, op0=ALU.mult),
        (SDPR.view,), (SDPR.view,))
    for ch in range(4):
        mm(PS[0][:, 0:512], B["ones1"].ap, SDPR[0:1, ch * 512:(ch + 1) * 512],
           (B["ones1"].view, SDPR.view), PS[0], True, True, True)
        vop(lambda e, ch=ch: e.tensor_copy(
            out=SDPR[:, ch * 512:(ch + 1) * 512], in_=PS[0][:, 0:512]),
            (PS[0].view,), (SDPR.view,))
    vop(lambda e: e.memset(STALL32.ap, 0.0), (), (STALL32.view,))
    vop(lambda e: e.memset(STALLBF.ap, 0.0), (), (STALLBF.view,))
    vop(lambda e: e.memset(B["tails"].ap, 0.0), (), (B["tails"].view,))

    for src, dst, nm in ((wq_td, wq_s, "wq_s"), (wk_td, wk_s, "wk_s"),
                         (wv_td, wv_s, "wv_s")):
        dma("gpsimd", dst, src, writes=(dvw[nm],))

    for oc in range(4):
        for half in range(2):
            ks = slice(half * 8, half * 8 + 8)
            dma("gpsimd", WOPR.ap, wo_td[oc, ks].rearrange("k p c -> p k c"),
                writes=(WOPR.view,))
            vop(lambda e: e.tensor_scalar(out=WOPR.ap, in0=WOPR.ap,
                                          scalar1=B["rmsc"].ap, scalar2=None,
                                          op0=ALU.mult),
                (WOPR.view, B["rmsc"].view), (WOPR.view,))
            dma("scalar", wo_s[oc, ks].rearrange("k p c -> p k c"), WOPR.ap,
                reads=(WOPR.view,), writes=(dvw["wo_s"],))

    for pi in range(3):
        for i in range(16):
            for j in range(4):
                cidx = pi * 64 + i * 4 + j
                vop(lambda e, j=j, cidx=cidx: e.tensor_scalar(
                    out=DGPR[:, j * 128:(j + 1) * 128], in0=B["i128"].ap,
                    scalar1=B["ccol"][:, cidx:cidx + 1], scalar2=None,
                    op0=ALU.mult),
                    (B["i128"].view, B["ccol"].view), (DGPR.view,))
            dma("scalar", dq_s[pi, :, i * 512:(i + 1) * 512], DGPR.ap,
                reads=(DGPR.view,), writes=(dvw["dq_s"],))

    def stage_a(src_ap, dstA, dst_view, G, M):
        for g in range(G):
            dma("gpsimd", SAI[:, 0:M], src_ap[g * 128:(g + 1) * 128, :],
                writes=(SAI.view,))
            for ch in range(M // 512):
                bank = PS[ch % 2]
                mm(bank[:, 0:512], B["h128"].ap, SAI[:, ch * 512:(ch + 1) * 512],
                   (B["h128"].view, SAI.view), bank, True, True, True)
                sop(lambda e, ch=ch, bank=bank: e.copy(
                    out=SAO[:, ch * 512:(ch + 1) * 512], in_=bank[:, 0:512]),
                    (bank.view,), (SAO.view,))
            dma("scalar", dstA[g * 128:(g + 1) * 128, :], SAO[:, 0:M],
                reads=(SAO.view,), writes=(dst_view,))

    stage_a(wup_d, upA_s, dvw["upA_s"], 16, INTER)
    stage_a(wg_d, gtA_s, dvw["gtA_s"], 16, INTER)
    stage_a(wd_d, dnA_s, dvw["dnA_s"], 64, D)

    def stage_b_ug(srcA, srcA_view, dst, dst_view):
        for j in range(16):
            src = srcA.rearrange("(g w) c -> g w c", w=128)[:, 8 * j:8 * j + 8, :]
            src = src.rearrange("g w c -> w g c")
            dma("gpsimd", SAI[:, 0:INTER], src, reads=(srcA_view,),
                writes=(SAI.view,))
            for ch in range(INTER // 512):
                bank = PS[ch % 2]
                mm(bank[:, 0:512], B["bd16"].ap, SAI[:, ch * 512:(ch + 1) * 512],
                   (B["bd16"].view, SAI.view), bank, True, True, True)
                vop(lambda e, ch=ch, bank=bank, j=j: e.tensor_scalar(
                    out=SAO[:, ch * 512:(ch + 1) * 512], in0=bank[:, 0:512],
                    scalar1=B["sucu"][:, j:j + 1], scalar2=None, op0=ALU.mult),
                    (bank.view, B["sucu"].view), (SAO.view,))
            for vg in range(8):
                dd = dst.rearrange("v w (g c) -> v w g c", c=128)[
                    vg * 8:vg * 8 + 8, 8 * j:8 * j + 8, :, :]
                dd = dd.rearrange("v w g c -> w g v c")
                srcb = SAO[:, vg * 1024:(vg + 1) * 1024].rearrange(
                    "p (v c) -> p v c", c=128)
                dma("scalar", dd, srcb, reads=(SAO.view,), writes=(dst_view,))

    stage_b_ug(upA_s, dvw["upA_s"], up_s, dvw["up_s"])
    stage_b_ug(gtA_s, dvw["gtA_s"], gt_s, dvw["gt_s"])

    for j in range(64):
        src = dnA_s.rearrange("(g w) c -> g w c", w=128)[:, 2 * j:2 * j + 2, :]
        src = src.rearrange("g w c -> w g c")
        dma("gpsimd", SAI[:, 0:D], src, reads=(dvw["dnA_s"],),
            writes=(SAI.view,))
        for ch in range(4):
            bank = PS[ch % 2]
            mm(bank[:, 0:512], B["bd64"].ap, SAI[:, ch * 512:(ch + 1) * 512],
               (B["bd64"].view, SAI.view), bank, True, True, True)
            vop(lambda e, ch=ch, bank=bank, j=j: e.scalar_tensor_tensor(
                out=SAO[:, ch * 512:(ch + 1) * 512], in0=bank[:, 0:512],
                scalar=B["sucd"][:, j:j + 1],
                in1=SDPR[:, ch * 512:(ch + 1) * 512],
                op0=ALU.mult, op1=ALU.mult),
                (bank.view, B["sucd"].view, SDPR.view), (SAO.view,))
        for q4 in range(4):
            dd = dn_s.rearrange("o p (v c) -> o p v c", c=128)[
                q4 * 4:q4 * 4 + 4, 2 * j:2 * j + 2, :, :]
            dd = dd.rearrange("o p v c -> p v o c")
            srcb = SAO[:, q4 * 512:(q4 + 1) * 512].rearrange(
                "p (o c) -> p o c", c=128)
            dma("scalar", dd, srcb, reads=(SAO.view,), writes=(dvw["dn_s"],))

    RAW = {0: RAWQ, 1: RAWK, 2: RAWV}
    W_S = {0: wq_s, 1: wk_s, 2: wv_s}
    WSV = {0: dvw["wq_s"], 1: dvw["wk_s"], 2: dvw["wv_s"]}

    def layernorm(srcbuf, dst_ap_fn, out_bf16):
        vop(lambda e: e.memset(STATS[:, 0:2], 0.0), (), (STATS.view,))
        sop(lambda e: e.activation(out=SQD.ap, in_=srcbuf.ap, func=AF.Copy,
                                   accum_out=STATS[:, 0:1]),
            (srcbuf.view,), (SQD.view, STATS.view))
        sop(lambda e: e.activation(out=SQD.ap, in_=srcbuf.ap, func=AF.Square,
                                   accum_out=STATS[:, 1:2]),
            (srcbuf.view,), (SQD.view, STATS.view))
        vop(lambda e: e.tensor_scalar(out=STATS[:, 2:3], in0=STATS[:, 0:1],
                                      scalar1=1.0 / 2048.0, scalar2=None,
                                      op0=ALU.mult),
            (STATS.view,), (STATS.view,))
        vop(lambda e: e.tensor_scalar(out=STATS[:, 3:4], in0=STATS[:, 1:2],
                                      scalar1=1.0 / 2048.0, scalar2=LN_EPS,
                                      op0=ALU.mult, op1=ALU.add),
            (STATS.view,), (STATS.view,))
        vop(lambda e: e.tensor_tensor(out=STATS[:, 4:5], in0=STATS[:, 2:3],
                                      in1=STATS[:, 2:3], op=ALU.mult),
            (STATS.view,), (STATS.view,))
        vop(lambda e: e.tensor_tensor(out=STATS[:, 5:6], in0=STATS[:, 3:4],
                                      in1=STATS[:, 4:5], op=ALU.subtract),
            (STATS.view,), (STATS.view,))
        vop(lambda e: e.reciprocal(out=STATS[:, 6:7], in_=STATS[:, 5:6]),
            (STATS.view,), (STATS.view,))
        sop(lambda e: e.activation(out=STATS[:, 7:8], in_=STATS[:, 6:7],
                                   func=AF.Sqrt),
            (STATS.view,), (STATS.view,))
        dst_ap, dst_views = dst_ap_fn()
        vop(lambda e, dst_ap=dst_ap: e.tensor_scalar(
            out=dst_ap, in0=srcbuf.ap, scalar1=STATS[:, 2:3],
            scalar2=STATS[:, 7:8], op0=ALU.subtract, op1=ALU.mult),
            (srcbuf.view, STATS.view), dst_views)

    for s in range(n_sc):
        b = s // 4
        sc = s % 4
        t0 = sc * SCT

        src = x_fm_d[b].rearrange("(i p) t -> p i t", p=128)[:, :, t0:t0 + SCT]
        dma("gpsimd", XFM.ap, src, writes=(XFM.view,))
        if sc == 0 and s > 0:
            vop(lambda e: e.memset(STALL32.ap, 0.0), (), (STALL32.view,))
            vop(lambda e: e.memset(STALLBF.ap, 0.0), (), (STALLBF.view,))

        for pi in range(3):
            raw = RAW[pi]
            for o in range(16):
                slot = o % 2
                dma("sync", WRING[:, slot * 2048:(slot + 1) * 2048],
                    W_S[pi][o], reads=(WSV[pi],), writes=(WRING.view,))
                bank = PS[o % 2]
                for i in range(16):
                    mm(bank[:, 0:512],
                       WRING[:, slot * 2048 + i * 128:slot * 2048 + (i + 1) * 128],
                       XFM[:, i * 512:(i + 1) * 512],
                       (WRING.view, XFM.view), bank, i == 0, i == 15, i == 15)
                sop(lambda e, raw=raw, o=o, bank=bank: e.copy(
                    out=raw[:, o * 516 + 4:o * 516 + 516], in_=bank[:, 0:512]),
                    (bank.view,), (raw.view,))
                tcol = pi * 64 + o * 4
                if sc == 0:
                    vop(lambda e, raw=raw, o=o: e.memset(
                        raw[:, o * 516:o * 516 + 4], 0.0), (), (raw.view,))
                else:
                    vop(lambda e, raw=raw, o=o, tcol=tcol: e.tensor_copy(
                        out=raw[:, o * 516:o * 516 + 4],
                        in_=B["tails"][:, tcol:tcol + 4]),
                        (B["tails"].view,), (raw.view,))
                vop(lambda e, raw=raw, o=o, tcol=tcol: e.tensor_copy(
                    out=B["tails"][:, tcol:tcol + 4],
                    in_=raw[:, o * 516 + 512:o * 516 + 516]),
                    (raw.view,), (B["tails"].view,))

        if attn_stage < 2:
            continue
        for tt in range(4):
            for i in range(16):
                mm(PS[4][:, tt * 16:(tt + 1) * 16],
                   XFM[:, i * 512 + tt * 128:i * 512 + (tt + 1) * 128],
                   B["wbb"][:, i * 16:(i + 1) * 16],
                   (XFM.view, B["wbb"].view), PS[4], i == 0, i == 15, i == 15)
        sop(lambda e: e.activation(out=B["beta"].ap, in_=PS[4][:, 0:64],
                                   func=AF.Sigmoid),
            (PS[4].view,), (B["beta"].view,))

        if attn_stage < 3:
            continue
        for pi in range(3):
            raw = RAW[pi]
            dma("sync", DQ.ap, dq_s[pi], reads=(dvw["dq_s"],), writes=(DQ.view,))
            for o in range(16):
                bank = PS[2 + o % 2]
                for j in range(4):
                    mm(bank[:, 0:512],
                       DQ[:, o * 512 + j * 128:o * 512 + (j + 1) * 128],
                       raw[:, o * 516 + 1 + j:o * 516 + 513 + j],
                       (DQ.view, raw.view), bank, j == 0, j == 3, j == 3)
                sop(lambda e, raw=raw, o=o, bank=bank: e.activation(
                    out=raw[:, o * 516 + 4:o * 516 + 516], in_=bank[:, 0:512],
                    func=AF.Silu),
                    (bank.view,), (raw.view,))

        if attn_stage < 4:
            continue
        for pi, nscale in ((0, 1.0 / 128.0), (1, 1.0)):
            raw = RAW[pi]
            half = pi * 512
            for h in range(16):
                sl = SQT[:, (h % 2) * 512:(h % 2) * 512 + 512]
                sop(lambda e, raw=raw, h=h, sl=sl: e.activation(
                    out=sl, in_=raw[:, h * 516 + 4:h * 516 + 516],
                    func=AF.Square),
                    (raw.view,), (SQT.view,))
                mm(PS[4][0:16, 0:512], B["esum"][:, 16 * h:16 * h + 16], sl,
                   (B["esum"].view, SQT.view), PS[4], h == 0, h == 15, True)
            if attn_stage < 41:
                continue
            vop(lambda e, half=half: e.tensor_scalar(
                out=RS32[:, half:half + 512], in0=PS[4][0:16, 0:512],
                scalar1=L2_EPS, scalar2=None, op0=ALU.add),
                (PS[4].view,), (RS32.view,))
            vop(lambda e, half=half: e.reciprocal(
                out=RS32[:, half:half + 512], in_=RS32[:, half:half + 512]),
                (RS32.view,), (RS32.view,))
            if attn_stage < 42:
                continue
            sop(lambda e, half=half, nscale=nscale: e.activation(
                out=RSBF[:, half:half + 512], in_=RS32[:, half:half + 512],
                func=AF.Sqrt, scale=nscale),
                (RS32.view,), (RSBF.view,))
            if attn_stage < 43:
                continue
            for h in range(16):
                mm(PS[5][:, 0:512], B["ebc"][:, 128 * h:128 * h + 128],
                   RSBF[:, half:half + 512],
                   (B["ebc"].view, RSBF.view), PS[5], True, True, True)
                if attn_stage < 44:
                    continue
                vop(lambda e, raw=raw, h=h: e.tensor_tensor(
                    out=raw[:, h * 516 + 4:h * 516 + 516],
                    in0=raw[:, h * 516 + 4:h * 516 + 516],
                    in1=PS[5][:, 0:512], op=ALU.mult),
                    (raw.view, PS[5].view), (raw.view,))

        if attn_stage < 50:
            continue
        for c in range(dlim[0]):
            cc = 4 + c * 128
            bcol = c * 16
            for hb in range(dlim[1]):
                hs = [4 * hb + k for k in range(4)]
                bhs = [b * 16 + h for h in hs]
                g = b * 4 + hb
                for k, h in enumerate(hs):
                    tpose(PT[0][:, k * 128:(k + 1) * 128],
                          RAWK[:, h * 516 + cc:h * 516 + cc + 128],
                          (RAWK.view,), PT[0], k == 3)
                vop(lambda e, hb=hb: e.tensor_copy(
                    out=KNTM[:, hb * 512:(hb + 1) * 512], in_=PT[0][:, 0:512]),
                    (PT[0].view,), (KNTM.view,))
                for k, h in enumerate(hs):
                    tpose(PT[0][:, 512 + k * 128:512 + (k + 1) * 128],
                          RAWV[:, h * 516 + cc:h * 516 + cc + 128],
                          (RAWV.view,), PT[0], k == 3)
                vop(lambda e, hb=hb: e.tensor_copy(
                    out=VTM[:, hb * 512:(hb + 1) * 512],
                    in_=PT[0][:, 512:1024]),
                    (PT[0].view,), (VTM.view,))
                for k, h in enumerate(hs):
                    mm(PS[3][:, k * 128:(k + 1) * 128],
                       RAWK[:, h * 516 + cc:h * 516 + cc + 128],
                       RAWK[:, h * 516 + cc:h * 516 + cc + 128],
                       (RAWK.view,), PS[3], True, True, k == 3)
                for k, h in enumerate(hs):
                    sop(lambda e, k=k, h=h, bcol=bcol: e.activation(
                        out=ANEG[:, k * 128:(k + 1) * 128],
                        in_=PS[3][:, k * 128:(k + 1) * 128], func=AF.Copy,
                        scale=B["beta"][:, bcol + h:bcol + h + 1]),
                        (PS[3].view, B["beta"].view), (ANEG.view,))
                vop(lambda e: e.tensor_tensor(out=ANEG.ap, in0=ANEG.ap,
                                              in1=B["mstneg"].ap, op=ALU.mult),
                    (ANEG.view, B["mstneg"].view), (ANEG.view,))
                for k in range(4):
                    tpose(PT[1][:, k * 128:(k + 1) * 128],
                          ANEG[:, k * 128:(k + 1) * 128],
                          (ANEG.view,), PT[1], k == 3)
                vop(lambda e: e.tensor_copy(out=BTS[:, 0:512],
                                            in_=PT[1][:, 0:512]),
                    (PT[1].view,), (BTS.view,))
                for lev in range(1, 7):
                    bt0 = (lev - 1) * 512

                    def bprev_ap(k, lev=lev):
                        if lev == 1:
                            return ANEG[:, k * 128:(k + 1) * 128]
                        off = ((lev - 1) % 2) * 512
                        return BPP[:, off + k * 128:off + (k + 1) * 128]

                    bprev_view = ANEG.view if lev == 1 else BPP.view
                    if lev < 6:
                        for k in range(4):
                            mm(PS[3][:, k * 128:(k + 1) * 128],
                               BTS[:, bt0 + k * 128:bt0 + (k + 1) * 128],
                               bprev_ap(k),
                               (BTS.view, bprev_view), PS[3], True, True,
                               k == 3)
                        off = (lev % 2) * 512
                        vop(lambda e, off=off: e.tensor_copy(
                            out=BPP[:, off:off + 512], in_=PS[3][:, 0:512]),
                            (PS[3].view,), (BPP.view,))
                    for k in range(4):
                        mm(PS[4][:, k * 128:(k + 1) * 128],
                           bprev_ap(k),
                           BTS[:, bt0 + k * 128:bt0 + (k + 1) * 128],
                           (BTS.view, bprev_view), PS[4], True, True, k == 3)
                    vop(lambda e, lev=lev: e.tensor_copy(
                        out=BTS[:, lev * 512:(lev + 1) * 512],
                        in_=PS[4][:, 0:512]),
                        (PS[4].view,), (BTS.view,))
                for k, (h, bh) in enumerate(zip(hs, bhs)):
                    mm(PS[5][:, k * 128:(k + 1) * 128],
                       RAWK[:, h * 516 + cc:h * 516 + cc + 128],
                       STBF[bh].ap,
                       (RAWK.view, STBF[bh].view), PS[5], True, True, k == 3)
                vop(lambda e, hb=hb: e.tensor_tensor(
                    out=XM.ap, in0=VTM[:, hb * 512:(hb + 1) * 512],
                    in1=PS[5][:, 0:512], op=ALU.subtract),
                    (VTM.view, PS[5].view), (XM.view,))
                for k, h in enumerate(hs):
                    vop(lambda e, k=k, h=h, bcol=bcol: e.tensor_scalar(
                        out=XM[:, k * 128:(k + 1) * 128],
                        in0=XM[:, k * 128:(k + 1) * 128],
                        scalar1=B["beta"][:, bcol + h:bcol + h + 1],
                        scalar2=None, op0=ALU.mult),
                        (XM.view, B["beta"].view), (XM.view,))
                vop(lambda e: e.tensor_copy(out=XB.ap, in_=XM.ap),
                    (XM.view,), (XB.view,))
                for lev in range(6, -1, -1):
                    for k in range(4):
                        mm(PS[5][:, k * 128:(k + 1) * 128],
                           BTS[:, lev * 512 + k * 128:lev * 512 + (k + 1) * 128],
                           XB[:, k * 128:(k + 1) * 128],
                           (BTS.view, XB.view), PS[5], True, True, k == 3)
                    vop(lambda e: e.tensor_tensor(out=XM.ap, in0=XM.ap,
                                                  in1=PS[5][:, 0:512],
                                                  op=ALU.add),
                        (XM.view, PS[5].view), (XM.view,))
                    vop(lambda e: e.tensor_copy(out=XB.ap, in_=XM.ap),
                        (XM.view,), (XB.view,))
                for k, h in enumerate(hs):
                    mm(PS[3][:, k * 128:(k + 1) * 128],
                       RAWK[:, h * 516 + cc:h * 516 + cc + 128],
                       RAWQ[:, h * 516 + cc:h * 516 + cc + 128],
                       (RAWK.view, RAWQ.view), PS[3], True, True, k == 3)
                vop(lambda e: e.tensor_tensor(out=MTB.ap, in0=PS[3][:, 0:512],
                                              in1=B["mtriu"].ap, op=ALU.mult),
                    (PS[3].view, B["mtriu"].view), (MTB.view,))
                for k, (h, bh) in enumerate(zip(hs, bhs)):
                    mm(PS[2][:, k * 128:(k + 1) * 128],
                       RAWQ[:, h * 516 + cc:h * 516 + cc + 128],
                       STBF[bh].ap,
                       (RAWQ.view, STBF[bh].view), PS[2], True, False, False)
                    mm(PS[2][:, k * 128:(k + 1) * 128],
                       MTB[:, k * 128:(k + 1) * 128],
                       XB[:, k * 128:(k + 1) * 128],
                       (MTB.view, XB.view), PS[2], False, True, k == 3)
                vop(lambda e: e.tensor_copy(out=OTMP.ap, in_=PS[2][:, 0:512]),
                    (PS[2].view,), (OTMP.view,))
                for k, (h, bh) in enumerate(zip(hs, bhs)):
                    mm(PS[5][:, k * 128:(k + 1) * 128],
                       KNTM[:, hb * 512 + k * 128:hb * 512 + (k + 1) * 128],
                       XB[:, k * 128:(k + 1) * 128],
                       (KNTM.view, XB.view), PS[5], True, True, k == 3)
                vop(lambda e, g=g: e.tensor_tensor(
                    out=STG32[g].ap, in0=STG32[g].ap, in1=PS[5][:, 0:512],
                    op=ALU.add),
                    (STG32[g].view, PS[5].view), (STG32[g].view,))
                vop(lambda e, g=g: e.tensor_copy(out=STGBF[g].ap,
                                                 in_=STG32[g].ap),
                    (STG32[g].view,), (STGBF[g].view,))
                vop(lambda e: e.tensor_tensor(out=XM.ap, in0=OTMP.ap,
                                              in1=OTMP.ap, op=ALU.mult),
                    (OTMP.view,), (XM.view,))
                vop(lambda e: e.reduce_sum(
                    out=RMSV[:, 0:4],
                    in_=XM.ap.rearrange("p (s c) -> p s c", c=128),
                    axis=mybir.AxisListType.X),
                    (XM.view,), (RMSV.view,))
                vop(lambda e: e.tensor_scalar(
                    out=RMSV[:, 0:4], in0=RMSV[:, 0:4], scalar1=1.0 / 128.0,
                    scalar2=RMS_EPS, op0=ALU.mult, op1=ALU.add),
                    (RMSV.view,), (RMSV.view,))
                vop(lambda e: e.reciprocal(out=RMSV[:, 4:8], in_=RMSV[:, 0:4]),
                    (RMSV.view,), (RMSV.view,))
                sop(lambda e: e.activation(out=RMSV[:, 8:12], in_=RMSV[:, 4:8],
                                           func=AF.Sqrt),
                    (RMSV.view,), (RMSV.view,))
                for k in range(4):
                    vop(lambda e, k=k: e.tensor_scalar(
                        out=XB[:, k * 128:(k + 1) * 128],
                        in0=OTMP[:, k * 128:(k + 1) * 128],
                        scalar1=RMSV[:, 8 + k:9 + k], scalar2=None,
                        op0=ALU.mult),
                        (OTMP.view, RMSV.view), (XB.view,))
                for k in range(4):
                    tpose(PT[1][:, k * 128:(k + 1) * 128],
                          XB[:, k * 128:(k + 1) * 128],
                          (XB.view,), PT[1], k == 3)
                vop(lambda e, hb=hb, c=c: e.tensor_copy(
                    out=ONFM.ap.rearrange("p (h t) -> p h t", t=512)[
                        :, 4 * hb:4 * hb + 4, c * 128:(c + 1) * 128],
                    in_=PT[1][:, 0:512].rearrange("p (h t) -> p h t", t=128)),
                    (PT[1].view,), (ONFM.view,))

        if trace:
            dma("scalar", trq_d[s], RAWQ.ap, reads=(RAWQ.view,), writes=(dvw["out"],))
            dma("scalar", trk_d[s], RAWK.ap, reads=(RAWK.view,), writes=(dvw["out"],))
            dma("scalar", trv_d[s], RAWV.ap, reads=(RAWV.view,), writes=(dvw["out"],))
            dma("scalar", tron_d[s], ONFM.ap, reads=(ONFM.view,), writes=(dvw["out"],))
            dma("scalar", trb_d[s], B["beta"].ap, reads=(B["beta"].view,), writes=(dvw["out"],))
        if attn_stage < 60:
            continue
        for pair in range(2):
            tts = (2 * pair, 2 * pair + 1)
            for ttl, tt in enumerate(tts):
                dma("scalar", OUTT[ttl].ap,
                    x_tm_d[b, t0 + tt * 128:t0 + (tt + 1) * 128, :],
                    writes=(OUTT[ttl].view,))
            for oc in range(4):
                for half in range(2):
                    dma("sync", WORING[:, half * 4096:(half + 1) * 4096],
                        wo_s[oc, half * 8:half * 8 + 8]
                        .rearrange("k p c -> p k c"),
                        reads=(dvw["wo_s"],), writes=(WORING.view,))
                for ttl, tt in enumerate(tts):
                    bank = PS[ttl]
                    for kk in range(16):
                        woff = (kk // 8) * 4096 + (kk % 8) * 512
                        mm(bank[:, 0:512],
                           ONFM[:, kk * 512 + tt * 128:kk * 512 + (tt + 1) * 128],
                           WORING[:, woff:woff + 512],
                           (ONFM.view, WORING.view), bank, kk == 0, kk == 15,
                           kk == 15)
                    vop(lambda e, ttl=ttl, oc=oc, bank=bank: e.tensor_tensor(
                        out=OUTT[ttl][:, oc * 512:(oc + 1) * 512],
                        in0=OUTT[ttl][:, oc * 512:(oc + 1) * 512],
                        in1=bank[:, 0:512], op=ALU.add),
                        (OUTT[ttl].view, bank.view), (OUTT[ttl].view,))
            for ttl, tt in enumerate(tts):
                layernorm(OUTT[ttl],
                          lambda tt=tt: (X1B[:, tt * 2048:(tt + 1) * 2048],
                                         (X1B.view,)),
                          True)
                for ig in range(4):
                    for k in range(4):
                        i = ig * 4 + k
                        tpose(PT[0][:, k * 128:(k + 1) * 128],
                              X1B[:, tt * 2048 + i * 128:tt * 2048 + (i + 1) * 128],
                              (X1B.view,), PT[0], k == 3)
                    vop(lambda e, ig=ig, tt=tt: e.tensor_copy(
                        out=X1FM.ap.rearrange("p (i t) -> p i t", t=512)[
                            :, ig * 4:ig * 4 + 4, tt * 128:(tt + 1) * 128],
                        in_=PT[0][:, 0:512].rearrange("p (i t) -> p i t",
                                                      t=128)),
                        (PT[0].view,), (X1FM.view,))

        if trace:
            dma("scalar", trx1_d[s], X1B.ap, reads=(X1B.view,), writes=(dvw["out"],))
            dma("scalar", trxf_d[s], X1FM.ap, reads=(X1FM.view,), writes=(dvw["out"],))
        if skip_ffn:
            continue
        for tc in range(2):
            for v in range(64):
                slot = v % 2
                dma("sync", UPR[:, slot * 2048:(slot + 1) * 2048], up_s[v],
                    reads=(dvw["up_s"],), writes=(UPR.view,))
                dma("sync", GTR[:, slot * 2048:(slot + 1) * 2048], gt_s[v],
                    reads=(dvw["gt_s"],), writes=(GTR.view,))
                for i in range(16):
                    mm(PS[0][:, 0:256],
                       UPR[:, slot * 2048 + i * 128:slot * 2048 + (i + 1) * 128],
                       X1FM[:, i * 512 + tc * 256:i * 512 + (tc + 1) * 256],
                       (UPR.view, X1FM.view), PS[0], i == 0, i == 15, i == 15)
                for i in range(16):
                    mm(PS[1][:, 0:256],
                       GTR[:, slot * 2048 + i * 128:slot * 2048 + (i + 1) * 128],
                       X1FM[:, i * 512 + tc * 256:i * 512 + (tc + 1) * 256],
                       (GTR.view, X1FM.view), PS[1], i == 0, i == 15, i == 15)
                sop(lambda e, v=v: e.activation(
                    out=SILU.ap, in_=PS[1][:, 0:256], func=AF.Silu,
                    scale=B["sgt"][:, v:v + 1]),
                    (PS[1].view, B["sgt"].view), (SILU.view,))
                vop(lambda e, v=v: e.scalar_tensor_tensor(
                    out=YB[:, v * 256:(v + 1) * 256], in0=PS[0][:, 0:256],
                    scalar=B["sup"][:, v:v + 1], in1=SILU.ap,
                    op0=ALU.mult, op1=ALU.mult),
                    (PS[0].view, B["sup"].view, SILU.view), (YB.view,))
            if trace and tc == 0:
                dma("scalar", try_d[s], YB.ap, reads=(YB.view,), writes=(dvw["out"],))
            for ocq in range(16):
                slot = ocq % 2
                dma("sync", WDR[:, slot * 8192:(slot + 1) * 8192], dn_s[ocq],
                    reads=(dvw["dn_s"],), writes=(WDR.view,))
                for ttl in range(2):
                    tt = tc * 2 + ttl
                    bank = PS[2 + ttl]
                    for v in range(64):
                        mm(bank[:, 0:128],
                           YB[:, v * 256 + ttl * 128:v * 256 + (ttl + 1) * 128],
                           WDR[:, slot * 8192 + v * 128:slot * 8192 + (v + 1) * 128],
                           (YB.view, WDR.view), bank, v == 0, v == 63, v == 63)
                    vop(lambda e, ttl=ttl, tt=tt, ocq=ocq, bank=bank:
                        e.scalar_tensor_tensor(
                            out=OUTT[ttl][:, ocq * 128:(ocq + 1) * 128],
                            in0=X1B[:, tt * 2048 + ocq * 128:
                                    tt * 2048 + (ocq + 1) * 128],
                            scalar=1.0, in1=bank[:, 0:128],
                            op0=ALU.mult, op1=ALU.add),
                        (X1B.view, bank.view), (OUTT[ttl].view,))
            if trace and tc == 0:
                dma("scalar", trh_d[s], sb["pf32"][0:128, 1024:5120],
                    reads=(OUTT[0].view, OUTT[1].view), writes=(dvw["out"],))
            for ttl in range(2):
                tt = tc * 2 + ttl
                layernorm(OUTT[ttl],
                          lambda ttl=ttl: (OUTT[ttl].ap, (OUTT[ttl].view,)),
                          False)
                dma("scalar", out_d[b, t0 + tt * 128:t0 + (tt + 1) * 128, :],
                    OUTT[ttl].ap, reads=(OUTT[ttl].view,),
                    writes=(dvw["out"],))

    for (oname, pool, col0, ncol, nrow) in dumps:
        isz = 2 if pool == "pbf" else 4
        dt = BF16 if pool == "pbf" else F32
        od = nc.dram_tensor(oname, [nrow, ncol], dt,
                            kind="ExternalOutput").ap()
        dbuf = _Buf(kb, sb[pool], col0, ncol, "dump_" + oname,
                    "bf" if pool == "pbf" else "f32", isz, nrow=nrow)
        dma("scalar", od, dbuf.ap, reads=(dbuf.view,), writes=(dvw["out"],))

    sems = {}
    for sname in kb.counts:
        sems[sname] = ctx.enter_context(nc.semaphore(sname))
    for sname in kb.counts:
        nc.gpsimd.sem_clear(sems[sname])
    nc.all_engine_barrier()
    with nc.Block() as block:
        _emit(kb, nc, block, sems)
    ctx.close()
    return nc


def _emit(kb, nc, block, sems):
    for eng_name in ("sync", "gpsimd", "vector", "scalar", "tensor"):
        ops = kb.ops[eng_name]
        if not ops:
            continue

        def section(eng, ops=ops, sems=sems):
            for waits, fn, stream, step in ops:
                for sname, v in waits.items():
                    eng.wait_ge(sems[sname], v)
                inst = fn(eng)
                if stream is not None:
                    inst.then_inc(sems[stream], step)

        getattr(block, eng_name)(section)

    def fence(eng, sems=sems, counts=dict(kb.counts)):
        for sname, cval in counts.items():
            if cval:
                eng.wait_ge(sems[sname], cval)

    block.sync(fence)


def _host_weights(inputs):
    w = {}
    Wq = np.asarray(inputs["L0_attn_Wq"], dtype=np.float32)
    Wk = np.asarray(inputs["L0_attn_Wk"], dtype=np.float32)
    Wv = np.asarray(inputs["L0_attn_Wv"], dtype=np.float32)
    Wo = np.asarray(inputs["L0_attn_Wo"], dtype=np.float32)
    for nm, W in (("wq_t", Wq), ("wk_t", Wk), ("wv_t", Wv)):
        a = W.reshape(16, 128, 16, 128).transpose(0, 3, 2, 1)
        w[nm] = np.ascontiguousarray(a.reshape(16, 128, 2048))
    a = Wo.reshape(4, 512, 16, 128).transpose(0, 2, 3, 1)
    w["wo_t"] = np.ascontiguousarray(a)
    w["wb_t"] = np.ascontiguousarray(
        np.asarray(inputs["L0_attn_Wb"], dtype=np.float32).T)
    cw = []
    for nm in ("conv_q", "conv_k", "conv_v"):
        cc = np.asarray(inputs["L0_attn_" + nm], dtype=np.float32)
        cw.append(cc.reshape(16, 128, 4).transpose(1, 0, 2).reshape(128, 64))
    w["conv_w"] = np.ascontiguousarray(np.stack(cw))
    w["rms_w"] = np.ascontiguousarray(
        np.asarray(inputs["L0_attn_rms_w"], dtype=np.float32).reshape(128, 1))
    w["wup_t"] = np.ascontiguousarray(
        np.asarray(inputs["L0_ffn_W_up"], dtype=np.float32).T)
    w["wg_t"] = np.ascontiguousarray(
        np.asarray(inputs["L0_ffn_W_gate"], dtype=np.float32).T)
    w["wd_t"] = np.ascontiguousarray(
        np.asarray(inputs["L0_ffn_W_down"], dtype=np.float32).T)
    w["su_ug"] = np.ascontiguousarray(
        np.asarray(inputs["L0_ffn_SU_ug"], dtype=np.float32).reshape(16, 128).T)
    w["su_dp"] = np.ascontiguousarray(
        np.asarray(inputs["L0_ffn_SU_dp"], dtype=np.float32).reshape(64, 128).T)
    ws = np.asarray(inputs["L0_ffn_Wscale_ug"], dtype=np.float32)
    w["ws_up"] = np.ascontiguousarray(ws[:INTER].reshape(64, 128).T)
    w["ws_gt"] = np.ascontiguousarray(ws[INTER:].reshape(64, 128).T)
    w["ws_dp"] = np.ascontiguousarray(
        np.asarray(inputs["L0_ffn_Wscale_dp"],
                   dtype=np.float32).reshape(1, 2048))
    for k, v in _host_consts().items():
        w["c_" + k] = v
    return w


def kernel(**inputs):
    nc = build_program()
    w = _host_weights(inputs)
    x = np.asarray(inputs["L0_attn_x"], dtype=np.float32)
    in_maps = []
    for core in range(NCORES):
        xs = np.ascontiguousarray(x[2 * core:2 * core + 2])
        m = dict(w)
        m["x_tm"] = xs
        m["x_fm"] = np.ascontiguousarray(xs.transpose(0, 2, 1))
        in_maps.append(m)
    res = run_bass_kernel_spmd(nc, in_maps, core_ids=list(range(NCORES)))
    return np.concatenate([r["out"] for r in res.results], axis=0)


# version 04
# speedup vs baseline: 1.0119x; 1.0119x over previous
import contextlib
import math
import numpy as np
import ml_dtypes
import concourse.bass as bass
import concourse.mybir as mybir
from concourse.bass_utils import run_bass_kernel_spmd

F32 = mybir.dt.float32
BF16 = mybir.dt.bfloat16
AF = mybir.ActivationFunctionType
ALU = mybir.AluOpType
BF = ml_dtypes.bfloat16

NB, SEQ, D, NH, HD, INTER = 2, 2048, 2048, 16, 128, 8192
NSC, SCT, NTT = 8, 512, 4
L2_EPS, RMS_EPS, LN_EPS = 1e-6, 1e-5, 1e-5
NCORES = 8


def _sylvester(n):
    H = np.array([[1.0]], dtype=np.float32)
    while H.shape[0] < n:
        H = np.block([[H, H], [H, -H]])
    return H.astype(np.float32)


def _host_consts():
    c = {}
    c["h128"] = _sylvester(128)
    h16 = _sylvester(16)
    bd16 = np.zeros((128, 128), dtype=np.float32)
    for j in range(8):
        bd16[16 * j:16 * j + 16, 16 * j:16 * j + 16] = h16
    c["bd16"] = bd16
    h64 = _sylvester(64)
    bd64 = np.zeros((128, 128), dtype=np.float32)
    for j in range(2):
        bd64[64 * j:64 * j + 64, 64 * j:64 * j + 64] = h64
    c["bd64"] = bd64
    c["i128"] = np.eye(128, dtype=np.float32)
    c["mstneg"] = np.tile(np.tril(np.ones((128, 128), np.float32), -1) * -1.0,
                          (1, 4))
    c["mtriu"] = np.tile(np.triu(np.ones((128, 128), np.float32), 0), (1, 4))
    esum = np.zeros((128, 256), dtype=np.float32)
    for h in range(16):
        esum[:, 16 * h + h] = 1.0
    c["esum"] = esum
    ebc = np.zeros((16, 2048), dtype=np.float32)
    for h in range(16):
        ebc[h, 128 * h:128 * h + 128] = 1.0
    c["ebc"] = ebc
    c["ones1"] = np.ones((1, 128), dtype=np.float32)
    return {k: v.astype(BF) for k, v in c.items()}


class _View:
    __slots__ = ("lo", "hi", "name", "w", "r", "overlaps", "space")

    def __init__(self, lo, hi, name, space):
        self.lo, self.hi = lo, hi
        self.name, self.space = name, space
        self.w = None
        self.r = {}
        self.overlaps = []


class _KB:
    ENG_STREAM = {"tensor": "pe", "vector": "dve", "scalar": "act",
                  "gpsimd": "pool", "sync": "spq"}
    DMA_STREAM = {"sync": "spq", "scalar": "actq", "gpsimd": "poolq"}

    QSUBS = 6

    def __init__(self, nc):
        self.nc = nc
        self.ops = {e: [] for e in ("tensor", "vector", "scalar", "gpsimd",
                                    "sync")}
        self.counts = {s: 0 for s in ("pe", "dve", "act", "pool")}
        for q in ("spq", "actq", "poolq"):
            for i in range(self.QSUBS):
                self.counts[q + str(i)] = 0
        self.qidx = {"spq": 0, "actq": 0, "poolq": 0}
        self.wm = {s: {} for s in self.counts}
        self.views = []

    def view(self, lo, hi, name, space):
        v = _View(lo, hi, name, space)
        for o in self.views:
            if o.space == space and not (hi <= o.lo or lo >= o.hi):
                o.overlaps.append(v)
                v.overlaps.append(o)
        self.views.append(v)
        return v

    paranoid = False

    def rec(self, eng, fn, reads=(), writes=(), dma=False, inc=True):
        if dma:
            qbase = self.DMA_STREAM[eng]
            stream = qbase + str(self.qidx[qbase] % self.QSUBS)
            self.qidx[qbase] += 1
        else:
            stream = self.ENG_STREAM[eng]
        step = 16 if dma else 1
        waits = {}
        wm = self.wm[stream]
        cur = self.counts[stream]

        def need(s, c):
            if s == stream and (stream == "pe" or c > cur):
                return
            if wm.get(s, 0) < c and waits.get(s, 0) < c:
                waits[s] = c

        if dma and cur > 0:
            need(stream, cur)

        for u in reads:
            for v in (u, *u.overlaps):
                if v.w:
                    need(*v.w)
        for u in writes:
            for v in (u, *u.overlaps):
                if v.w:
                    need(*v.w)
                for s, c in v.r.items():
                    need(s, c)
        if inc:
            cnt = self.counts[stream] + step
            self.counts[stream] = cnt
        else:
            assert stream == "pe"
            cnt = self.counts[stream] + 1
        for s, c in waits.items():
            wm[s] = c
        for u in reads:
            if u.r.get(stream, 0) < cnt:
                u.r[stream] = cnt
        for u in writes:
            u.w = (stream, cnt)
            u.r = {}
        self.ops[eng].append((waits, fn, stream if inc else None, step))


class _Buf:
    __slots__ = ("t", "col0", "ncol", "nrow", "row0", "view")

    def __init__(self, kb, tensor, col0, ncol, name, space, itemsz,
                 nrow=128, row0=0):
        self.t, self.col0, self.ncol = tensor, col0, ncol
        self.nrow, self.row0 = nrow, row0
        self.view = kb.view(col0 * itemsz, (col0 + ncol) * itemsz, name, space)

    def __getitem__(self, sl):
        if isinstance(sl, tuple):
            rows, cols = sl
        else:
            rows, cols = slice(None), sl
        r0 = self.row0 if rows.start is None else self.row0 + rows.start
        r1 = self.row0 + self.nrow if rows.stop is None else self.row0 + rows.stop
        c0 = self.col0 if cols.start is None else self.col0 + cols.start
        c1 = self.col0 + self.ncol if cols.stop is None else self.col0 + cols.stop
        return self.t[r0:r1, c0:c1]

    @property
    def ap(self):
        return self.t[self.row0:self.row0 + self.nrow,
                      self.col0:self.col0 + self.ncol]


def build_program(cfg=None):
    cfg = cfg or {}
    n_sc = cfg.get("n_sc", NSC)
    dumps = cfg.get("dumps", [])
    skip_ffn = cfg.get("skip_ffn", False)
    attn_stage = cfg.get("attn_stage", 99)
    dlim = cfg.get("delta_limit", (4, 4))
    nc = bass.Bass("TRN2", target_bir_lowering=False)
    kb = _KB(nc)

    def din(name, shape, dt=F32):
        return nc.dram_tensor(name, list(shape), dt, kind="ExternalInput").ap()

    x_tm_d = din("x_tm", (NB, SEQ, D))
    x_fm_d = din("x_fm", (NB, D, SEQ))
    wq_td = din("wq_t", (16, 128, 2048))
    wk_td = din("wk_t", (16, 128, 2048))
    wv_td = din("wv_t", (16, 128, 2048))
    wb_td = din("wb_t", (D, NH))
    wo_td = din("wo_t", (4, 16, 128, 512))
    conv_d = din("conv_w", (3, 128, 64))
    rms_d = din("rms_w", (128, 1))
    wup_d = din("wup_t", (D, INTER))
    wg_d = din("wg_t", (D, INTER))
    wd_d = din("wd_t", (INTER, D))
    suug_d = din("su_ug", (128, 16))
    sudp_d = din("su_dp", (128, 64))
    wsup_d = din("ws_up", (128, 64))
    wsgt_d = din("ws_gt", (128, 64))
    wsdp_d = din("ws_dp", (1, 2048))
    consts_np = _host_consts()
    cst_d = {k: din("c_" + k, v.shape, BF16) for k, v in consts_np.items()}
    out_d = nc.dram_tensor("out", [NB, SEQ, D], F32, kind="ExternalOutput").ap()
    trace = cfg.get("trace", False)
    if trace:
        trq_d = nc.dram_tensor("tr_q", [NSC, 128, 8256], BF16, kind="ExternalOutput").ap()
        trk_d = nc.dram_tensor("tr_k", [NSC, 128, 8256], BF16, kind="ExternalOutput").ap()
        trv_d = nc.dram_tensor("tr_v", [NSC, 128, 8256], BF16, kind="ExternalOutput").ap()
        tron_d = nc.dram_tensor("tr_on", [NSC, 128, 8192], BF16, kind="ExternalOutput").ap()
        trx1_d = nc.dram_tensor("tr_x1", [NSC, 128, 8192], BF16, kind="ExternalOutput").ap()
        trb_d = nc.dram_tensor("tr_b", [NSC, 128, 64], F32, kind="ExternalOutput").ap()
        trxf_d = nc.dram_tensor("tr_x1fm", [NSC, 128, 8192], BF16, kind="ExternalOutput").ap()
        try_d = nc.dram_tensor("tr_y", [NSC, 128, 16384], BF16, kind="ExternalOutput").ap()
        trh_d = nc.dram_tensor("tr_h2", [NSC, 128, 4096], F32, kind="ExternalOutput").ap()

    expose = cfg.get("expose", ())

    def dscr(name, shape, dt=BF16):
        kind = "ExternalOutput" if name in expose else "Internal"
        return nc.dram_tensor(name, list(shape), dt, kind=kind).ap()

    wq_s = dscr("wq_s", (16, 128, 2048))
    wk_s = dscr("wk_s", (16, 128, 2048))
    wv_s = dscr("wv_s", (16, 128, 2048))
    wo_s = dscr("wo_s", (4, 16, 128, 512))
    dq_s = dscr("dq_s", (3, 128, 8192))
    upA_s = dscr("upA_s", (D, INTER))
    gtA_s = dscr("gtA_s", (D, INTER))
    dnA_s = dscr("dnA_s", (INTER, D))
    up_s = dscr("up_s", (64, 128, 2048))
    gt_s = dscr("gt_s", (64, 128, 2048))
    dn_s = dscr("dn_s", (16, 128, 8192))

    ctx = contextlib.ExitStack()
    sb = {}

    def alloc(name, shape, dt):
        t = ctx.enter_context(nc.sbuf_tensor(name, list(shape), dt))
        sb[name] = t
        return t

    for k, v in consts_np.items():
        alloc("k_" + k, v.shape, BF16)
    alloc("ccol", (128, 192), F32)
    alloc("rmsc", (128, 1), F32)
    alloc("sup", (128, 64), F32)
    alloc("sgt", (128, 64), F32)
    alloc("sucu", (128, 16), F32)
    alloc("sucd", (128, 64), F32)
    alloc("wbb", (128, 256), BF16)
    alloc("tails", (128, 192), BF16)
    alloc("st32", (128, 4096), F32)
    alloc("stbf", (128, 4096), BF16)
    alloc("beta", (128, 64), F32)
    PBF_COLS = 61888
    PF32_COLS = 8300
    alloc("pbf", (128, PBF_COLS), BF16)
    alloc("pf32", (128, PF32_COLS), F32)
    pbanks = [ctx.enter_context(nc.psum_tensor("ps%d" % i, [128, 512], F32))
              for i in range(6)]
    ptb = [ctx.enter_context(nc.psum_tensor("pt%d" % i, [128, 1024], BF16))
           for i in range(2)]

    B = {}
    for k, v in consts_np.items():
        B[k] = _Buf(kb, sb["k_" + k], 0, v.shape[1], k, "t_" + k, 2,
                    nrow=v.shape[0])
    for nm, isz in (("ccol", 4), ("rmsc", 4), ("sup", 4), ("sgt", 4),
                    ("sucu", 4), ("sucd", 4), ("wbb", 2), ("tails", 2),
                    ("beta", 4)):
        t = sb[nm]
        B[nm] = _Buf(kb, t, 0, t.shape[1], nm, "t_" + nm, isz, nrow=t.shape[0])

    ST32 = [_Buf(kb, sb["st32"], bh * 128, 128, "st32_%d" % bh, "t_st32", 4)
            for bh in range(32)]
    STBF = [_Buf(kb, sb["stbf"], bh * 128, 128, "stbf_%d" % bh, "t_stbf", 2)
            for bh in range(32)]
    STG32 = [_Buf(kb, sb["st32"], g * 512, 512, "stg32_%d" % g, "t_st32", 4)
             for g in range(8)]
    STGBF = [_Buf(kb, sb["stbf"], g * 512, 512, "stgbf_%d" % g, "t_stbf", 2)
             for g in range(8)]
    STALL32 = _Buf(kb, sb["st32"], 0, 4096, "st32_all", "t_st32", 4)
    STALLBF = _Buf(kb, sb["stbf"], 0, 4096, "stbf_all", "t_stbf", 2)

    class _PBank:
        def __init__(self, t, view):
            self.t, self.view = t, view

        def __getitem__(self, sl):
            return self.t[sl]

    PS = [_PBank(pbanks[i], kb.view(0, 1, "ps%d" % i, "zps%d" % i))
          for i in range(6)]
    PT = [_PBank(ptb[i], kb.view(0, 1, "pt%d" % i, "zpt%d" % i))
          for i in range(2)]

    dvw = {}
    for nm in ("wq_s", "wk_s", "wv_s", "wo_s", "dq_s", "upA_s", "gtA_s",
               "dnA_s", "up_s", "gt_s", "dn_s", "out"):
        dvw[nm] = kb.view(0, 1, nm, "dram_" + nm)

    def dma(q, out_ap, in_ap, reads=(), writes=()):
        kb.rec(q, lambda e, o=out_ap, i=in_ap: e.dma_start(out=o, in_=i),
               reads=reads, writes=writes, dma=True)

    def mm(out_ap, lhsT_ap, rhs_ap, reads, wbank, start, stop, inc):
        kb.rec("tensor",
               lambda e, o=out_ap, l=lhsT_ap, r=rhs_ap, s1=start, s2=stop:
               e.matmul(o, l, r, start=s1, stop=s2),
               reads=reads, writes=(wbank.view,), inc=inc)

    def tpose(out_ap, in_ap, reads, wbank, inc):
        kb.rec("tensor",
               lambda e, o=out_ap, i=in_ap: e.transpose(o, i, B["i128"].ap),
               reads=tuple(reads) + (B["i128"].view,), writes=(wbank.view,),
               inc=inc)

    def vop(fn, reads, writes):
        kb.rec("vector", fn, reads=reads, writes=writes)

    def sop(fn, reads, writes):
        kb.rec("scalar", fn, reads=reads, writes=writes)

    def bfb(col0, ncol, name, nrow=128, row0=0):
        assert col0 + ncol <= PBF_COLS, (name, col0 + ncol)
        return _Buf(kb, sb["pbf"], col0, ncol, name, "bf", 2, nrow, row0)

    def f32b(col0, ncol, name, nrow=128, row0=0):
        assert col0 + ncol <= PF32_COLS, (name, col0 + ncol)
        return _Buf(kb, sb["pf32"], col0, ncol, name, "f32", 4, nrow, row0)

    XFM = bfb(0, 8192, "xfm")
    RAWQ = bfb(8192, 8256, "rawq")
    RAWK = bfb(16448, 8256, "rawk")
    RAWV = bfb(24704, 8256, "rawv")
    KNTM = bfb(32960, 2048, "kn_tm")
    VTM = bfb(35008, 2048, "v_tm")
    XB = bfb(37056, 512, "xb")
    ANEG = bfb(37568, 512, "aneg")
    BTS = bfb(38080, 3584, "bts")
    BPP = bfb(41664, 1024, "b_pp")
    MTB = bfb(42688, 512, "mt")
    SQT = bfb(43200, 1024, "sqtmp")
    RSBF = bfb(44224, 1024, "rsbf", nrow=16)
    ONFM = bfb(45248, 8192, "onfm")
    WRING = bfb(53440, 4096, "wring")
    WORING = bfb(53440, 8192, "woring")
    DQ = bfb(0, 8192, "dq")
    YB = bfb(8192, 16384, "y")
    X1B = bfb(24704, 8192, "x1b")
    X1FM = bfb(0, 8192, "x1fm")
    UPR = bfb(32960, 4096, "upring")
    GTR = bfb(37056, 4096, "gatering")
    WDR = bfb(45248, 16384, "wdring")
    SILU = bfb(43200, 256, "silu_tmp")
    SILU2 = bfb(43456, 256, "silu_tmp2")
    SQD = bfb(8192, 2048, "sqdump")
    SDPR = bfb(41664, 2048, "sdprep")
    SAI = bfb(0, 8192, "stgA_in")
    SAO = bfb(8192, 8192, "stgA_out")
    WOPR = bfb(16448, 4096, "wo_prep")
    DGPR = bfb(20544, 512, "diag_prep")
    XM = f32b(0, 512, "xm")
    OTMP = f32b(512, 512, "o_tmp")
    OUTB = f32b(1024, 4096, "outbuf")
    OUTT = [_Buf(kb, sb["pf32"], 1024 + ttl * 2048, 2048, "outb%d" % ttl,
                 "f32", 4) for ttl in range(2)]
    RS32 = f32b(5120, 1024, "rs32", nrow=16)
    SUC = f32b(6144, 1, "su_col")
    STATS = f32b(6145, 16, "stats")
    RMSV = f32b(6161, 16, "rmsv")

    for nm in consts_np:
        dma("sync", B[nm].ap, cst_d[nm], writes=(B[nm].view,))
    dma("sync", B["rmsc"].ap, rms_d, writes=(B["rmsc"].view,))
    for pi in range(3):
        dma("sync", B["ccol"][:, pi * 64:(pi + 1) * 64], conv_d[pi],
            writes=(B["ccol"].view,))
    dma("gpsimd", B["wbb"].ap, wb_td.rearrange("(i p) h -> p i h", p=128),
        writes=(B["wbb"].view,))
    dma("sync", B["sup"].ap, wsup_d, writes=(B["sup"].view,))
    dma("sync", B["sgt"].ap, wsgt_d, writes=(B["sgt"].view,))
    c2048 = 1.0 / math.sqrt(2048.0)
    c8192 = 1.0 / math.sqrt(8192.0)
    vop(lambda e: e.tensor_scalar(out=B["sup"].ap, in0=B["sup"].ap,
                                  scalar1=c2048, scalar2=None, op0=ALU.mult),
        (B["sup"].view,), (B["sup"].view,))
    vop(lambda e: e.tensor_scalar(out=B["sgt"].ap, in0=B["sgt"].ap,
                                  scalar1=c2048, scalar2=None, op0=ALU.mult),
        (B["sgt"].view,), (B["sgt"].view,))
    for j in range(16):
        dma("sync", B["sucu"][:, j:j + 1], suug_d[8 * j:8 * j + 8, :],
            writes=(B["sucu"].view,))
    for j in range(64):
        dma("sync", B["sucd"][:, j:j + 1], sudp_d[2 * j:2 * j + 2, :],
            writes=(B["sucd"].view,))
    dma("gpsimd", SDPR[0:1, 0:2048], wsdp_d, writes=(SDPR.view,))
    vop(lambda e: e.tensor_scalar(out=SDPR[0:1, 0:2048], in0=SDPR[0:1, 0:2048],
                                  scalar1=c8192, scalar2=None, op0=ALU.mult),
        (SDPR.view,), (SDPR.view,))
    for ch in range(4):
        mm(PS[0][:, 0:512], B["ones1"].ap, SDPR[0:1, ch * 512:(ch + 1) * 512],
           (B["ones1"].view, SDPR.view), PS[0], True, True, True)
        vop(lambda e, ch=ch: e.tensor_copy(
            out=SDPR[:, ch * 512:(ch + 1) * 512], in_=PS[0][:, 0:512]),
            (PS[0].view,), (SDPR.view,))
    vop(lambda e: e.memset(STALL32.ap, 0.0), (), (STALL32.view,))
    vop(lambda e: e.memset(STALLBF.ap, 0.0), (), (STALLBF.view,))
    vop(lambda e: e.memset(B["tails"].ap, 0.0), (), (B["tails"].view,))

    for src, dst, nm in ((wq_td, wq_s, "wq_s"), (wk_td, wk_s, "wk_s"),
                         (wv_td, wv_s, "wv_s")):
        dma("gpsimd", dst, src, writes=(dvw[nm],))

    for oc in range(4):
        for half in range(2):
            ks = slice(half * 8, half * 8 + 8)
            dma("gpsimd", WOPR.ap, wo_td[oc, ks].rearrange("k p c -> p k c"),
                writes=(WOPR.view,))
            vop(lambda e: e.tensor_scalar(out=WOPR.ap, in0=WOPR.ap,
                                          scalar1=B["rmsc"].ap, scalar2=None,
                                          op0=ALU.mult),
                (WOPR.view, B["rmsc"].view), (WOPR.view,))
            dma("scalar", wo_s[oc, ks].rearrange("k p c -> p k c"), WOPR.ap,
                reads=(WOPR.view,), writes=(dvw["wo_s"],))

    for pi in range(3):
        for i in range(16):
            for j in range(4):
                cidx = pi * 64 + i * 4 + j
                vop(lambda e, j=j, cidx=cidx: e.tensor_scalar(
                    out=DGPR[:, j * 128:(j + 1) * 128], in0=B["i128"].ap,
                    scalar1=B["ccol"][:, cidx:cidx + 1], scalar2=None,
                    op0=ALU.mult),
                    (B["i128"].view, B["ccol"].view), (DGPR.view,))
            dma("scalar", dq_s[pi, :, i * 512:(i + 1) * 512], DGPR.ap,
                reads=(DGPR.view,), writes=(dvw["dq_s"],))

    def stage_a(src_ap, dstA, dst_view, G, M):
        for g in range(G):
            dma("gpsimd", SAI[:, 0:M], src_ap[g * 128:(g + 1) * 128, :],
                writes=(SAI.view,))
            for ch in range(M // 512):
                bank = PS[ch % 2]
                mm(bank[:, 0:512], B["h128"].ap, SAI[:, ch * 512:(ch + 1) * 512],
                   (B["h128"].view, SAI.view), bank, True, True, True)
                sop(lambda e, ch=ch, bank=bank: e.copy(
                    out=SAO[:, ch * 512:(ch + 1) * 512], in_=bank[:, 0:512]),
                    (bank.view,), (SAO.view,))
            dma("scalar", dstA[g * 128:(g + 1) * 128, :], SAO[:, 0:M],
                reads=(SAO.view,), writes=(dst_view,))

    stage_a(wup_d, upA_s, dvw["upA_s"], 16, INTER)
    stage_a(wg_d, gtA_s, dvw["gtA_s"], 16, INTER)
    stage_a(wd_d, dnA_s, dvw["dnA_s"], 64, D)

    def stage_b_ug(srcA, srcA_view, dst, dst_view):
        for j in range(16):
            src = srcA.rearrange("(g w) c -> g w c", w=128)[:, 8 * j:8 * j + 8, :]
            src = src.rearrange("g w c -> w g c")
            dma("gpsimd", SAI[:, 0:INTER], src, reads=(srcA_view,),
                writes=(SAI.view,))
            for ch in range(INTER // 512):
                bank = PS[ch % 2]
                mm(bank[:, 0:512], B["bd16"].ap, SAI[:, ch * 512:(ch + 1) * 512],
                   (B["bd16"].view, SAI.view), bank, True, True, True)
                vop(lambda e, ch=ch, bank=bank, j=j: e.tensor_scalar(
                    out=SAO[:, ch * 512:(ch + 1) * 512], in0=bank[:, 0:512],
                    scalar1=B["sucu"][:, j:j + 1], scalar2=None, op0=ALU.mult),
                    (bank.view, B["sucu"].view), (SAO.view,))
            for vg in range(8):
                dd = dst.rearrange("v w (g c) -> v w g c", c=128)[
                    vg * 8:vg * 8 + 8, 8 * j:8 * j + 8, :, :]
                dd = dd.rearrange("v w g c -> w g v c")
                srcb = SAO[:, vg * 1024:(vg + 1) * 1024].rearrange(
                    "p (v c) -> p v c", c=128)
                dma("scalar", dd, srcb, reads=(SAO.view,), writes=(dst_view,))

    stage_b_ug(upA_s, dvw["upA_s"], up_s, dvw["up_s"])
    stage_b_ug(gtA_s, dvw["gtA_s"], gt_s, dvw["gt_s"])

    for j in range(64):
        src = dnA_s.rearrange("(g w) c -> g w c", w=128)[:, 2 * j:2 * j + 2, :]
        src = src.rearrange("g w c -> w g c")
        dma("gpsimd", SAI[:, 0:D], src, reads=(dvw["dnA_s"],),
            writes=(SAI.view,))
        for ch in range(4):
            bank = PS[ch % 2]
            mm(bank[:, 0:512], B["bd64"].ap, SAI[:, ch * 512:(ch + 1) * 512],
               (B["bd64"].view, SAI.view), bank, True, True, True)
            vop(lambda e, ch=ch, bank=bank, j=j: e.scalar_tensor_tensor(
                out=SAO[:, ch * 512:(ch + 1) * 512], in0=bank[:, 0:512],
                scalar=B["sucd"][:, j:j + 1],
                in1=SDPR[:, ch * 512:(ch + 1) * 512],
                op0=ALU.mult, op1=ALU.mult),
                (bank.view, B["sucd"].view, SDPR.view), (SAO.view,))
        for q4 in range(4):
            dd = dn_s.rearrange("o p (v c) -> o p v c", c=128)[
                q4 * 4:q4 * 4 + 4, 2 * j:2 * j + 2, :, :]
            dd = dd.rearrange("o p v c -> p v o c")
            srcb = SAO[:, q4 * 512:(q4 + 1) * 512].rearrange(
                "p (o c) -> p o c", c=128)
            dma("scalar", dd, srcb, reads=(SAO.view,), writes=(dvw["dn_s"],))

    RAW = {0: RAWQ, 1: RAWK, 2: RAWV}
    W_S = {0: wq_s, 1: wk_s, 2: wv_s}
    WSV = {0: dvw["wq_s"], 1: dvw["wk_s"], 2: dvw["wv_s"]}

    def layernorm(srcbuf, dst_ap_fn, out_bf16):
        vop(lambda e: e.memset(STATS[:, 0:2], 0.0), (), (STATS.view,))
        sop(lambda e: e.activation(out=SQD.ap, in_=srcbuf.ap, func=AF.Copy,
                                   accum_out=STATS[:, 0:1]),
            (srcbuf.view,), (SQD.view, STATS.view))
        sop(lambda e: e.activation(out=SQD.ap, in_=srcbuf.ap, func=AF.Square,
                                   accum_out=STATS[:, 1:2]),
            (srcbuf.view,), (SQD.view, STATS.view))
        vop(lambda e: e.tensor_scalar(out=STATS[:, 2:3], in0=STATS[:, 0:1],
                                      scalar1=1.0 / 2048.0, scalar2=None,
                                      op0=ALU.mult),
            (STATS.view,), (STATS.view,))
        vop(lambda e: e.tensor_scalar(out=STATS[:, 3:4], in0=STATS[:, 1:2],
                                      scalar1=1.0 / 2048.0, scalar2=LN_EPS,
                                      op0=ALU.mult, op1=ALU.add),
            (STATS.view,), (STATS.view,))
        vop(lambda e: e.tensor_tensor(out=STATS[:, 4:5], in0=STATS[:, 2:3],
                                      in1=STATS[:, 2:3], op=ALU.mult),
            (STATS.view,), (STATS.view,))
        vop(lambda e: e.tensor_tensor(out=STATS[:, 5:6], in0=STATS[:, 3:4],
                                      in1=STATS[:, 4:5], op=ALU.subtract),
            (STATS.view,), (STATS.view,))
        vop(lambda e: e.reciprocal(out=STATS[:, 6:7], in_=STATS[:, 5:6]),
            (STATS.view,), (STATS.view,))
        sop(lambda e: e.activation(out=STATS[:, 7:8], in_=STATS[:, 6:7],
                                   func=AF.Sqrt),
            (STATS.view,), (STATS.view,))
        dst_ap, dst_views = dst_ap_fn()
        vop(lambda e, dst_ap=dst_ap: e.tensor_scalar(
            out=dst_ap, in0=srcbuf.ap, scalar1=STATS[:, 2:3],
            scalar2=STATS[:, 7:8], op0=ALU.subtract, op1=ALU.mult),
            (srcbuf.view, STATS.view), dst_views)

    for s in range(n_sc):
        b = s // 4
        sc = s % 4
        t0 = sc * SCT

        src = x_fm_d[b].rearrange("(i p) t -> p i t", p=128)[:, :, t0:t0 + SCT]
        dma("gpsimd", XFM.ap, src, writes=(XFM.view,))
        if sc == 0 and s > 0:
            vop(lambda e: e.memset(STALL32.ap, 0.0), (), (STALL32.view,))
            vop(lambda e: e.memset(STALLBF.ap, 0.0), (), (STALLBF.view,))

        for pi in range(3):
            raw = RAW[pi]
            for o in range(16):
                slot = o % 2
                dma("sync", WRING[:, slot * 2048:(slot + 1) * 2048],
                    W_S[pi][o], reads=(WSV[pi],), writes=(WRING.view,))
                bank = PS[o % 4]
                for i in range(16):
                    mm(bank[:, 0:512],
                       WRING[:, slot * 2048 + i * 128:slot * 2048 + (i + 1) * 128],
                       XFM[:, i * 512:(i + 1) * 512],
                       (WRING.view, XFM.view), bank, i == 0, i == 15, i == 15)
                sop(lambda e, raw=raw, o=o, bank=bank: e.copy(
                    out=raw[:, o * 516 + 4:o * 516 + 516], in_=bank[:, 0:512]),
                    (bank.view,), (raw.view,))
                tcol = pi * 64 + o * 4
                if sc == 0:
                    vop(lambda e, raw=raw, o=o: e.memset(
                        raw[:, o * 516:o * 516 + 4], 0.0), (), (raw.view,))
                else:
                    vop(lambda e, raw=raw, o=o, tcol=tcol: e.tensor_copy(
                        out=raw[:, o * 516:o * 516 + 4],
                        in_=B["tails"][:, tcol:tcol + 4]),
                        (B["tails"].view,), (raw.view,))
                vop(lambda e, raw=raw, o=o, tcol=tcol: e.tensor_copy(
                    out=B["tails"][:, tcol:tcol + 4],
                    in_=raw[:, o * 516 + 512:o * 516 + 516]),
                    (raw.view,), (B["tails"].view,))

        if attn_stage < 2:
            continue
        for tt in range(4):
            for i in range(16):
                mm(PS[4][:, tt * 16:(tt + 1) * 16],
                   XFM[:, i * 512 + tt * 128:i * 512 + (tt + 1) * 128],
                   B["wbb"][:, i * 16:(i + 1) * 16],
                   (XFM.view, B["wbb"].view), PS[4], i == 0, i == 15, i == 15)
        sop(lambda e: e.activation(out=B["beta"].ap, in_=PS[4][:, 0:64],
                                   func=AF.Sigmoid),
            (PS[4].view,), (B["beta"].view,))

        if attn_stage < 3:
            continue
        for pi in range(3):
            raw = RAW[pi]
            dma("sync", DQ.ap, dq_s[pi], reads=(dvw["dq_s"],), writes=(DQ.view,))
            for o in range(16):
                bank = PS[4 + o % 2]
                for j in range(4):
                    mm(bank[:, 0:512],
                       DQ[:, o * 512 + j * 128:o * 512 + (j + 1) * 128],
                       raw[:, o * 516 + 1 + j:o * 516 + 513 + j],
                       (DQ.view, raw.view), bank, j == 0, j == 3, j == 3)
                sop(lambda e, raw=raw, o=o, bank=bank: e.activation(
                    out=raw[:, o * 516 + 4:o * 516 + 516], in_=bank[:, 0:512],
                    func=AF.Silu),
                    (bank.view,), (raw.view,))

        if attn_stage < 4:
            continue
        for pi, nscale in ((0, 1.0 / 128.0), (1, 1.0)):
            raw = RAW[pi]
            half = pi * 512
            for h in range(16):
                sl = SQT[:, (h % 2) * 512:(h % 2) * 512 + 512]
                sop(lambda e, raw=raw, h=h, sl=sl: e.activation(
                    out=sl, in_=raw[:, h * 516 + 4:h * 516 + 516],
                    func=AF.Square),
                    (raw.view,), (SQT.view,))
                mm(PS[4][0:16, 0:512], B["esum"][:, 16 * h:16 * h + 16], sl,
                   (B["esum"].view, SQT.view), PS[4], h == 0, h == 15, True)
            if attn_stage < 41:
                continue
            vop(lambda e, half=half: e.tensor_scalar(
                out=RS32[:, half:half + 512], in0=PS[4][0:16, 0:512],
                scalar1=L2_EPS, scalar2=None, op0=ALU.add),
                (PS[4].view,), (RS32.view,))
            vop(lambda e, half=half: e.reciprocal(
                out=RS32[:, half:half + 512], in_=RS32[:, half:half + 512]),
                (RS32.view,), (RS32.view,))
            if attn_stage < 42:
                continue
            sop(lambda e, half=half, nscale=nscale: e.activation(
                out=RSBF[:, half:half + 512], in_=RS32[:, half:half + 512],
                func=AF.Sqrt, scale=nscale),
                (RS32.view,), (RSBF.view,))
            if attn_stage < 43:
                continue
            for h in range(16):
                mm(PS[5][:, 0:512], B["ebc"][:, 128 * h:128 * h + 128],
                   RSBF[:, half:half + 512],
                   (B["ebc"].view, RSBF.view), PS[5], True, True, True)
                if attn_stage < 44:
                    continue
                vop(lambda e, raw=raw, h=h: e.tensor_tensor(
                    out=raw[:, h * 516 + 4:h * 516 + 516],
                    in0=raw[:, h * 516 + 4:h * 516 + 516],
                    in1=PS[5][:, 0:512], op=ALU.mult),
                    (raw.view, PS[5].view), (raw.view,))

        if attn_stage < 50:
            continue
        for c in range(dlim[0]):
            cc = 4 + c * 128
            bcol = c * 16
            for hb in range(dlim[1]):
                hs = [4 * hb + k for k in range(4)]
                bhs = [b * 16 + h for h in hs]
                g = b * 4 + hb
                for k, h in enumerate(hs):
                    tpose(PT[0][:, k * 128:(k + 1) * 128],
                          RAWK[:, h * 516 + cc:h * 516 + cc + 128],
                          (RAWK.view,), PT[0], k == 3)
                vop(lambda e, hb=hb: e.tensor_copy(
                    out=KNTM[:, hb * 512:(hb + 1) * 512], in_=PT[0][:, 0:512]),
                    (PT[0].view,), (KNTM.view,))
                for k, h in enumerate(hs):
                    tpose(PT[0][:, 512 + k * 128:512 + (k + 1) * 128],
                          RAWV[:, h * 516 + cc:h * 516 + cc + 128],
                          (RAWV.view,), PT[0], k == 3)
                vop(lambda e, hb=hb: e.tensor_copy(
                    out=VTM[:, hb * 512:(hb + 1) * 512],
                    in_=PT[0][:, 512:1024]),
                    (PT[0].view,), (VTM.view,))
                for k, h in enumerate(hs):
                    mm(PS[3][:, k * 128:(k + 1) * 128],
                       RAWK[:, h * 516 + cc:h * 516 + cc + 128],
                       RAWK[:, h * 516 + cc:h * 516 + cc + 128],
                       (RAWK.view,), PS[3], True, True, k == 3)
                for k, h in enumerate(hs):
                    sop(lambda e, k=k, h=h, bcol=bcol: e.activation(
                        out=ANEG[:, k * 128:(k + 1) * 128],
                        in_=PS[3][:, k * 128:(k + 1) * 128], func=AF.Copy,
                        scale=B["beta"][:, bcol + h:bcol + h + 1]),
                        (PS[3].view, B["beta"].view), (ANEG.view,))
                vop(lambda e: e.tensor_tensor(out=ANEG.ap, in0=ANEG.ap,
                                              in1=B["mstneg"].ap, op=ALU.mult),
                    (ANEG.view, B["mstneg"].view), (ANEG.view,))
                for k in range(4):
                    tpose(PT[1][:, k * 128:(k + 1) * 128],
                          ANEG[:, k * 128:(k + 1) * 128],
                          (ANEG.view,), PT[1], k == 3)
                vop(lambda e: e.tensor_copy(out=BTS[:, 0:512],
                                            in_=PT[1][:, 0:512]),
                    (PT[1].view,), (BTS.view,))
                for lev in range(1, 7):
                    bt0 = (lev - 1) * 512

                    def bprev_ap(k, lev=lev):
                        if lev == 1:
                            return ANEG[:, k * 128:(k + 1) * 128]
                        off = ((lev - 1) % 2) * 512
                        return BPP[:, off + k * 128:off + (k + 1) * 128]

                    bprev_view = ANEG.view if lev == 1 else BPP.view
                    if lev < 6:
                        for k in range(4):
                            mm(PS[3][:, k * 128:(k + 1) * 128],
                               BTS[:, bt0 + k * 128:bt0 + (k + 1) * 128],
                               bprev_ap(k),
                               (BTS.view, bprev_view), PS[3], True, True,
                               k == 3)
                        off = (lev % 2) * 512
                        vop(lambda e, off=off: e.tensor_copy(
                            out=BPP[:, off:off + 512], in_=PS[3][:, 0:512]),
                            (PS[3].view,), (BPP.view,))
                    for k in range(4):
                        mm(PS[4][:, k * 128:(k + 1) * 128],
                           bprev_ap(k),
                           BTS[:, bt0 + k * 128:bt0 + (k + 1) * 128],
                           (BTS.view, bprev_view), PS[4], True, True, k == 3)
                    vop(lambda e, lev=lev: e.tensor_copy(
                        out=BTS[:, lev * 512:(lev + 1) * 512],
                        in_=PS[4][:, 0:512]),
                        (PS[4].view,), (BTS.view,))
                for k, (h, bh) in enumerate(zip(hs, bhs)):
                    mm(PS[5][:, k * 128:(k + 1) * 128],
                       RAWK[:, h * 516 + cc:h * 516 + cc + 128],
                       STBF[bh].ap,
                       (RAWK.view, STBF[bh].view), PS[5], True, True, k == 3)
                vop(lambda e, hb=hb: e.tensor_tensor(
                    out=XM.ap, in0=VTM[:, hb * 512:(hb + 1) * 512],
                    in1=PS[5][:, 0:512], op=ALU.subtract),
                    (VTM.view, PS[5].view), (XM.view,))
                for k, h in enumerate(hs):
                    vop(lambda e, k=k, h=h, bcol=bcol: e.tensor_scalar(
                        out=XM[:, k * 128:(k + 1) * 128],
                        in0=XM[:, k * 128:(k + 1) * 128],
                        scalar1=B["beta"][:, bcol + h:bcol + h + 1],
                        scalar2=None, op0=ALU.mult),
                        (XM.view, B["beta"].view), (XM.view,))
                vop(lambda e: e.tensor_copy(out=XB.ap, in_=XM.ap),
                    (XM.view,), (XB.view,))
                for lev in range(6, -1, -1):
                    for k in range(4):
                        mm(PS[5][:, k * 128:(k + 1) * 128],
                           BTS[:, lev * 512 + k * 128:lev * 512 + (k + 1) * 128],
                           XB[:, k * 128:(k + 1) * 128],
                           (BTS.view, XB.view), PS[5], True, True, k == 3)
                    vop(lambda e: e.tensor_tensor(out=XM.ap, in0=XM.ap,
                                                  in1=PS[5][:, 0:512],
                                                  op=ALU.add),
                        (XM.view, PS[5].view), (XM.view,))
                    vop(lambda e: e.tensor_copy(out=XB.ap, in_=XM.ap),
                        (XM.view,), (XB.view,))
                for k, h in enumerate(hs):
                    mm(PS[3][:, k * 128:(k + 1) * 128],
                       RAWK[:, h * 516 + cc:h * 516 + cc + 128],
                       RAWQ[:, h * 516 + cc:h * 516 + cc + 128],
                       (RAWK.view, RAWQ.view), PS[3], True, True, k == 3)
                vop(lambda e: e.tensor_tensor(out=MTB.ap, in0=PS[3][:, 0:512],
                                              in1=B["mtriu"].ap, op=ALU.mult),
                    (PS[3].view, B["mtriu"].view), (MTB.view,))
                for k, (h, bh) in enumerate(zip(hs, bhs)):
                    mm(PS[2][:, k * 128:(k + 1) * 128],
                       RAWQ[:, h * 516 + cc:h * 516 + cc + 128],
                       STBF[bh].ap,
                       (RAWQ.view, STBF[bh].view), PS[2], True, False, False)
                    mm(PS[2][:, k * 128:(k + 1) * 128],
                       MTB[:, k * 128:(k + 1) * 128],
                       XB[:, k * 128:(k + 1) * 128],
                       (MTB.view, XB.view), PS[2], False, True, k == 3)
                vop(lambda e: e.tensor_copy(out=OTMP.ap, in_=PS[2][:, 0:512]),
                    (PS[2].view,), (OTMP.view,))
                for k, (h, bh) in enumerate(zip(hs, bhs)):
                    mm(PS[5][:, k * 128:(k + 1) * 128],
                       KNTM[:, hb * 512 + k * 128:hb * 512 + (k + 1) * 128],
                       XB[:, k * 128:(k + 1) * 128],
                       (KNTM.view, XB.view), PS[5], True, True, k == 3)
                vop(lambda e, g=g: e.tensor_tensor(
                    out=STG32[g].ap, in0=STG32[g].ap, in1=PS[5][:, 0:512],
                    op=ALU.add),
                    (STG32[g].view, PS[5].view), (STG32[g].view,))
                vop(lambda e, g=g: e.tensor_copy(out=STGBF[g].ap,
                                                 in_=STG32[g].ap),
                    (STG32[g].view,), (STGBF[g].view,))
                vop(lambda e: e.tensor_tensor(out=XM.ap, in0=OTMP.ap,
                                              in1=OTMP.ap, op=ALU.mult),
                    (OTMP.view,), (XM.view,))
                vop(lambda e: e.reduce_sum(
                    out=RMSV[:, 0:4],
                    in_=XM.ap.rearrange("p (s c) -> p s c", c=128),
                    axis=mybir.AxisListType.X),
                    (XM.view,), (RMSV.view,))
                vop(lambda e: e.tensor_scalar(
                    out=RMSV[:, 0:4], in0=RMSV[:, 0:4], scalar1=1.0 / 128.0,
                    scalar2=RMS_EPS, op0=ALU.mult, op1=ALU.add),
                    (RMSV.view,), (RMSV.view,))
                vop(lambda e: e.reciprocal(out=RMSV[:, 4:8], in_=RMSV[:, 0:4]),
                    (RMSV.view,), (RMSV.view,))
                sop(lambda e: e.activation(out=RMSV[:, 8:12], in_=RMSV[:, 4:8],
                                           func=AF.Sqrt),
                    (RMSV.view,), (RMSV.view,))
                for k in range(4):
                    vop(lambda e, k=k: e.tensor_scalar(
                        out=XB[:, k * 128:(k + 1) * 128],
                        in0=OTMP[:, k * 128:(k + 1) * 128],
                        scalar1=RMSV[:, 8 + k:9 + k], scalar2=None,
                        op0=ALU.mult),
                        (OTMP.view, RMSV.view), (XB.view,))
                for k in range(4):
                    tpose(PT[1][:, k * 128:(k + 1) * 128],
                          XB[:, k * 128:(k + 1) * 128],
                          (XB.view,), PT[1], k == 3)
                vop(lambda e, hb=hb, c=c: e.tensor_copy(
                    out=ONFM.ap.rearrange("p (h t) -> p h t", t=512)[
                        :, 4 * hb:4 * hb + 4, c * 128:(c + 1) * 128],
                    in_=PT[1][:, 0:512].rearrange("p (h t) -> p h t", t=128)),
                    (PT[1].view,), (ONFM.view,))

        if trace:
            dma("scalar", trq_d[s], RAWQ.ap, reads=(RAWQ.view,), writes=(dvw["out"],))
            dma("scalar", trk_d[s], RAWK.ap, reads=(RAWK.view,), writes=(dvw["out"],))
            dma("scalar", trv_d[s], RAWV.ap, reads=(RAWV.view,), writes=(dvw["out"],))
            dma("scalar", tron_d[s], ONFM.ap, reads=(ONFM.view,), writes=(dvw["out"],))
            dma("scalar", trb_d[s], B["beta"].ap, reads=(B["beta"].view,), writes=(dvw["out"],))
        if attn_stage < 60:
            continue
        for pair in range(2):
            tts = (2 * pair, 2 * pair + 1)
            for ttl, tt in enumerate(tts):
                dma("scalar", OUTT[ttl].ap,
                    x_tm_d[b, t0 + tt * 128:t0 + (tt + 1) * 128, :],
                    writes=(OUTT[ttl].view,))
            for oc in range(4):
                for half in range(2):
                    dma("sync", WORING[:, half * 4096:(half + 1) * 4096],
                        wo_s[oc, half * 8:half * 8 + 8]
                        .rearrange("k p c -> p k c"),
                        reads=(dvw["wo_s"],), writes=(WORING.view,))
                for ttl, tt in enumerate(tts):
                    bank = PS[(oc * 2 + ttl) % 4]
                    for kk in range(16):
                        woff = (kk // 8) * 4096 + (kk % 8) * 512
                        mm(bank[:, 0:512],
                           ONFM[:, kk * 512 + tt * 128:kk * 512 + (tt + 1) * 128],
                           WORING[:, woff:woff + 512],
                           (ONFM.view, WORING.view), bank, kk == 0, kk == 15,
                           kk == 15)
                    vop(lambda e, ttl=ttl, oc=oc, bank=bank: e.tensor_tensor(
                        out=OUTT[ttl][:, oc * 512:(oc + 1) * 512],
                        in0=OUTT[ttl][:, oc * 512:(oc + 1) * 512],
                        in1=bank[:, 0:512], op=ALU.add),
                        (OUTT[ttl].view, bank.view), (OUTT[ttl].view,))
            for ttl, tt in enumerate(tts):
                layernorm(OUTT[ttl],
                          lambda tt=tt: (X1B[:, tt * 2048:(tt + 1) * 2048],
                                         (X1B.view,)),
                          True)
                for ig in range(4):
                    for k in range(4):
                        i = ig * 4 + k
                        tpose(PT[0][:, k * 128:(k + 1) * 128],
                              X1B[:, tt * 2048 + i * 128:tt * 2048 + (i + 1) * 128],
                              (X1B.view,), PT[0], k == 3)
                    vop(lambda e, ig=ig, tt=tt: e.tensor_copy(
                        out=X1FM.ap.rearrange("p (i t) -> p i t", t=512)[
                            :, ig * 4:ig * 4 + 4, tt * 128:(tt + 1) * 128],
                        in_=PT[0][:, 0:512].rearrange("p (i t) -> p i t",
                                                      t=128)),
                        (PT[0].view,), (X1FM.view,))

        if trace:
            dma("scalar", trx1_d[s], X1B.ap, reads=(X1B.view,), writes=(dvw["out"],))
            dma("scalar", trxf_d[s], X1FM.ap, reads=(X1FM.view,), writes=(dvw["out"],))
        if skip_ffn:
            continue
        for tc in range(2):
            for v in range(64):
                slot = v % 2
                dma("sync", UPR[:, slot * 2048:(slot + 1) * 2048], up_s[v],
                    reads=(dvw["up_s"],), writes=(UPR.view,))
                dma("sync", GTR[:, slot * 2048:(slot + 1) * 2048], gt_s[v],
                    reads=(dvw["gt_s"],), writes=(GTR.view,))
                ub = PS[(v % 2) * 2]
                gb = PS[(v % 2) * 2 + 1]
                for i in range(16):
                    mm(ub[:, 0:256],
                       UPR[:, slot * 2048 + i * 128:slot * 2048 + (i + 1) * 128],
                       X1FM[:, i * 512 + tc * 256:i * 512 + (tc + 1) * 256],
                       (UPR.view, X1FM.view), ub, i == 0, i == 15, i == 15)
                for i in range(16):
                    mm(gb[:, 0:256],
                       GTR[:, slot * 2048 + i * 128:slot * 2048 + (i + 1) * 128],
                       X1FM[:, i * 512 + tc * 256:i * 512 + (tc + 1) * 256],
                       (GTR.view, X1FM.view), gb, i == 0, i == 15, i == 15)
                sil = SILU if v % 2 == 0 else SILU2
                sop(lambda e, v=v, gb=gb, sil=sil: e.activation(
                    out=sil.ap, in_=gb[:, 0:256], func=AF.Silu,
                    scale=B["sgt"][:, v:v + 1]),
                    (gb.view, B["sgt"].view), (sil.view,))
                vop(lambda e, v=v, ub=ub, sil=sil: e.scalar_tensor_tensor(
                    out=YB[:, v * 256:(v + 1) * 256], in0=ub[:, 0:256],
                    scalar=B["sup"][:, v:v + 1], in1=sil.ap,
                    op0=ALU.mult, op1=ALU.mult),
                    (ub.view, B["sup"].view, sil.view), (YB.view,))
            if trace and tc == 0:
                dma("scalar", try_d[s], YB.ap, reads=(YB.view,), writes=(dvw["out"],))
            for ocq in range(16):
                slot = ocq % 2
                dma("sync", WDR[:, slot * 8192:(slot + 1) * 8192], dn_s[ocq],
                    reads=(dvw["dn_s"],), writes=(WDR.view,))
                for ttl in range(2):
                    tt = tc * 2 + ttl
                    bank = PS[2 + (ocq * 2 + ttl) % 4]
                    for v in range(64):
                        mm(bank[:, 0:128],
                           YB[:, v * 256 + ttl * 128:v * 256 + (ttl + 1) * 128],
                           WDR[:, slot * 8192 + v * 128:slot * 8192 + (v + 1) * 128],
                           (YB.view, WDR.view), bank, v == 0, v == 63, v == 63)
                    vop(lambda e, ttl=ttl, tt=tt, ocq=ocq, bank=bank:
                        e.scalar_tensor_tensor(
                            out=OUTT[ttl][:, ocq * 128:(ocq + 1) * 128],
                            in0=X1B[:, tt * 2048 + ocq * 128:
                                    tt * 2048 + (ocq + 1) * 128],
                            scalar=1.0, in1=bank[:, 0:128],
                            op0=ALU.mult, op1=ALU.add),
                        (X1B.view, bank.view), (OUTT[ttl].view,))
            if trace and tc == 0:
                dma("scalar", trh_d[s], sb["pf32"][0:128, 1024:5120],
                    reads=(OUTT[0].view, OUTT[1].view), writes=(dvw["out"],))
            for ttl in range(2):
                tt = tc * 2 + ttl
                layernorm(OUTT[ttl],
                          lambda ttl=ttl: (OUTT[ttl].ap, (OUTT[ttl].view,)),
                          False)
                dma("scalar", out_d[b, t0 + tt * 128:t0 + (tt + 1) * 128, :],
                    OUTT[ttl].ap, reads=(OUTT[ttl].view,),
                    writes=(dvw["out"],))

    for (oname, pool, col0, ncol, nrow) in dumps:
        isz = 2 if pool == "pbf" else 4
        dt = BF16 if pool == "pbf" else F32
        od = nc.dram_tensor(oname, [nrow, ncol], dt,
                            kind="ExternalOutput").ap()
        dbuf = _Buf(kb, sb[pool], col0, ncol, "dump_" + oname,
                    "bf" if pool == "pbf" else "f32", isz, nrow=nrow)
        dma("scalar", od, dbuf.ap, reads=(dbuf.view,), writes=(dvw["out"],))

    sems = {}
    for sname in kb.counts:
        sems[sname] = ctx.enter_context(nc.semaphore(sname))
    for sname in kb.counts:
        nc.gpsimd.sem_clear(sems[sname])
    nc.all_engine_barrier()
    with nc.Block() as block:
        _emit(kb, nc, block, sems)
    ctx.close()
    return nc


def _emit(kb, nc, block, sems):
    for eng_name in ("sync", "gpsimd", "vector", "scalar", "tensor"):
        ops = kb.ops[eng_name]
        if not ops:
            continue

        def section(eng, ops=ops, sems=sems):
            for waits, fn, stream, step in ops:
                for sname, v in waits.items():
                    eng.wait_ge(sems[sname], v)
                inst = fn(eng)
                if stream is not None:
                    inst.then_inc(sems[stream], step)

        getattr(block, eng_name)(section)

    def fence(eng, sems=sems, counts=dict(kb.counts)):
        for sname, cval in counts.items():
            if cval:
                eng.wait_ge(sems[sname], cval)

    block.sync(fence)


def _host_weights(inputs):
    w = {}
    Wq = np.asarray(inputs["L0_attn_Wq"], dtype=np.float32)
    Wk = np.asarray(inputs["L0_attn_Wk"], dtype=np.float32)
    Wv = np.asarray(inputs["L0_attn_Wv"], dtype=np.float32)
    Wo = np.asarray(inputs["L0_attn_Wo"], dtype=np.float32)
    for nm, W in (("wq_t", Wq), ("wk_t", Wk), ("wv_t", Wv)):
        a = W.reshape(16, 128, 16, 128).transpose(0, 3, 2, 1)
        w[nm] = np.ascontiguousarray(a.reshape(16, 128, 2048))
    a = Wo.reshape(4, 512, 16, 128).transpose(0, 2, 3, 1)
    w["wo_t"] = np.ascontiguousarray(a)
    w["wb_t"] = np.ascontiguousarray(
        np.asarray(inputs["L0_attn_Wb"], dtype=np.float32).T)
    cw = []
    for nm in ("conv_q", "conv_k", "conv_v"):
        cc = np.asarray(inputs["L0_attn_" + nm], dtype=np.float32)
        cw.append(cc.reshape(16, 128, 4).transpose(1, 0, 2).reshape(128, 64))
    w["conv_w"] = np.ascontiguousarray(np.stack(cw))
    w["rms_w"] = np.ascontiguousarray(
        np.asarray(inputs["L0_attn_rms_w"], dtype=np.float32).reshape(128, 1))
    w["wup_t"] = np.ascontiguousarray(
        np.asarray(inputs["L0_ffn_W_up"], dtype=np.float32).T)
    w["wg_t"] = np.ascontiguousarray(
        np.asarray(inputs["L0_ffn_W_gate"], dtype=np.float32).T)
    w["wd_t"] = np.ascontiguousarray(
        np.asarray(inputs["L0_ffn_W_down"], dtype=np.float32).T)
    w["su_ug"] = np.ascontiguousarray(
        np.asarray(inputs["L0_ffn_SU_ug"], dtype=np.float32).reshape(16, 128).T)
    w["su_dp"] = np.ascontiguousarray(
        np.asarray(inputs["L0_ffn_SU_dp"], dtype=np.float32).reshape(64, 128).T)
    ws = np.asarray(inputs["L0_ffn_Wscale_ug"], dtype=np.float32)
    w["ws_up"] = np.ascontiguousarray(ws[:INTER].reshape(64, 128).T)
    w["ws_gt"] = np.ascontiguousarray(ws[INTER:].reshape(64, 128).T)
    w["ws_dp"] = np.ascontiguousarray(
        np.asarray(inputs["L0_ffn_Wscale_dp"],
                   dtype=np.float32).reshape(1, 2048))
    for k, v in _host_consts().items():
        w["c_" + k] = v
    return w


def kernel(**inputs):
    nc = build_program()
    w = _host_weights(inputs)
    x = np.asarray(inputs["L0_attn_x"], dtype=np.float32)
    in_maps = []
    for core in range(NCORES):
        xs = np.ascontiguousarray(x[2 * core:2 * core + 2])
        m = dict(w)
        m["x_tm"] = xs
        m["x_fm"] = np.ascontiguousarray(xs.transpose(0, 2, 1))
        in_maps.append(m)
    res = run_bass_kernel_spmd(nc, in_maps, core_ids=list(range(NCORES)))
    return np.concatenate([r["out"] for r in res.results], axis=0)


# version 05 (final)
# speedup vs baseline: 1.0652x; 1.0527x over previous
import contextlib
import math
import numpy as np
import ml_dtypes
import concourse.bass as bass
import concourse.mybir as mybir
from concourse.bass_utils import run_bass_kernel_spmd

F32 = mybir.dt.float32
BF16 = mybir.dt.bfloat16
AF = mybir.ActivationFunctionType
ALU = mybir.AluOpType
BF = ml_dtypes.bfloat16

NB, SEQ, D, NH, HD, INTER = 2, 2048, 2048, 16, 128, 8192
NSC, SCT, NTT = 8, 512, 4
L2_EPS, RMS_EPS, LN_EPS = 1e-6, 1e-5, 1e-5
NCORES = 8


def _sylvester(n):
    H = np.array([[1.0]], dtype=np.float32)
    while H.shape[0] < n:
        H = np.block([[H, H], [H, -H]])
    return H.astype(np.float32)


def _host_consts():
    c = {}
    c["h128"] = _sylvester(128)
    h16 = _sylvester(16)
    bd16 = np.zeros((128, 128), dtype=np.float32)
    for j in range(8):
        bd16[16 * j:16 * j + 16, 16 * j:16 * j + 16] = h16
    c["bd16"] = bd16
    h64 = _sylvester(64)
    bd64 = np.zeros((128, 128), dtype=np.float32)
    for j in range(2):
        bd64[64 * j:64 * j + 64, 64 * j:64 * j + 64] = h64
    c["bd64"] = bd64
    c["i128"] = np.eye(128, dtype=np.float32)
    c["mstneg"] = np.tile(np.tril(np.ones((128, 128), np.float32), -1) * -1.0,
                          (1, 4))
    c["mtriu"] = np.tile(np.triu(np.ones((128, 128), np.float32), 0), (1, 4))
    esum = np.zeros((128, 256), dtype=np.float32)
    for h in range(16):
        esum[:, 16 * h + h] = 1.0
    c["esum"] = esum
    ebc = np.zeros((16, 2048), dtype=np.float32)
    for h in range(16):
        ebc[h, 128 * h:128 * h + 128] = 1.0
    c["ebc"] = ebc
    c["ones1"] = np.ones((1, 128), dtype=np.float32)
    return {k: v.astype(BF) for k, v in c.items()}


class _View:
    __slots__ = ("lo", "hi", "name", "w", "r", "overlaps", "space")

    def __init__(self, lo, hi, name, space):
        self.lo, self.hi = lo, hi
        self.name, self.space = name, space
        self.w = None
        self.r = {}
        self.overlaps = []


class _KB:
    ENG_STREAM = {"tensor": "pe", "vector": "dve", "scalar": "act",
                  "gpsimd": "pool", "sync": "spq"}
    DMA_STREAM = {"sync": "spq", "scalar": "actq", "gpsimd": "poolq"}

    QSUBS = 6

    def __init__(self, nc):
        self.nc = nc
        self.ops = {e: [] for e in ("tensor", "vector", "scalar", "gpsimd",
                                    "sync")}
        self.counts = {s: 0 for s in ("pe", "dve", "act", "pool")}
        for q in ("spq", "actq", "poolq"):
            for i in range(self.QSUBS):
                self.counts[q + str(i)] = 0
        self.qidx = {"spq": 0, "actq": 0, "poolq": 0}
        self.wm = {s: {} for s in self.counts}
        self.views = []

    def view(self, lo, hi, name, space):
        v = _View(lo, hi, name, space)
        for o in self.views:
            if o.space == space and not (hi <= o.lo or lo >= o.hi):
                o.overlaps.append(v)
                v.overlaps.append(o)
        self.views.append(v)
        return v

    paranoid = False

    def rec(self, eng, fn, reads=(), writes=(), dma=False, inc=True):
        if dma:
            qbase = self.DMA_STREAM[eng]
            stream = qbase + str(self.qidx[qbase] % self.QSUBS)
            self.qidx[qbase] += 1
        else:
            stream = self.ENG_STREAM[eng]
        step = 16 if dma else 1
        waits = {}
        wm = self.wm[stream]
        cur = self.counts[stream]

        def need(s, c):
            if s == stream and (stream == "pe" or c > cur):
                return
            if wm.get(s, 0) < c and waits.get(s, 0) < c:
                waits[s] = c

        if dma and cur > 0:
            need(stream, cur)

        for u in reads:
            for v in (u, *u.overlaps):
                if v.w:
                    need(*v.w)
        for u in writes:
            for v in (u, *u.overlaps):
                if v.w:
                    need(*v.w)
                for s, c in v.r.items():
                    need(s, c)
        if inc:
            cnt = self.counts[stream] + step
            self.counts[stream] = cnt
        else:
            assert stream == "pe"
            cnt = self.counts[stream] + 1
        for s, c in waits.items():
            wm[s] = c
        for u in reads:
            if u.r.get(stream, 0) < cnt:
                u.r[stream] = cnt
        for u in writes:
            u.w = (stream, cnt)
            u.r = {}
        self.ops[eng].append((waits, fn, stream if inc else None, step))


class _Buf:
    __slots__ = ("t", "col0", "ncol", "nrow", "row0", "view")

    def __init__(self, kb, tensor, col0, ncol, name, space, itemsz,
                 nrow=128, row0=0):
        self.t, self.col0, self.ncol = tensor, col0, ncol
        self.nrow, self.row0 = nrow, row0
        self.view = kb.view(col0 * itemsz, (col0 + ncol) * itemsz, name, space)

    def __getitem__(self, sl):
        if isinstance(sl, tuple):
            rows, cols = sl
        else:
            rows, cols = slice(None), sl
        r0 = self.row0 if rows.start is None else self.row0 + rows.start
        r1 = self.row0 + self.nrow if rows.stop is None else self.row0 + rows.stop
        c0 = self.col0 if cols.start is None else self.col0 + cols.start
        c1 = self.col0 + self.ncol if cols.stop is None else self.col0 + cols.stop
        return self.t[r0:r1, c0:c1]

    @property
    def ap(self):
        return self.t[self.row0:self.row0 + self.nrow,
                      self.col0:self.col0 + self.ncol]


def build_program(cfg=None):
    cfg = cfg or {}
    n_sc = cfg.get("n_sc", NSC)
    dumps = cfg.get("dumps", [])
    skip_ffn = cfg.get("skip_ffn", False)
    attn_stage = cfg.get("attn_stage", 99)
    dlim = cfg.get("delta_limit", (4, 4))
    nc = bass.Bass("TRN2", target_bir_lowering=False)
    kb = _KB(nc)

    def din(name, shape, dt=F32):
        return nc.dram_tensor(name, list(shape), dt, kind="ExternalInput").ap()

    x_tm_d = din("x_tm", (NB, SEQ, D))
    x_fm_d = din("x_fm", (NB, D, SEQ))
    wq_td = din("wq_t", (16, 128, 2048))
    wk_td = din("wk_t", (16, 128, 2048))
    wv_td = din("wv_t", (16, 128, 2048))
    wb_td = din("wb_t", (D, NH))
    wo_td = din("wo_t", (4, 16, 128, 512))
    conv_d = din("conv_w", (3, 128, 64))
    rms_d = din("rms_w", (128, 1))
    wup_d = din("wup_t", (D, INTER))
    wg_d = din("wg_t", (D, INTER))
    wd_d = din("wd_t", (INTER, D))
    suug_d = din("su_ug", (128, 16))
    sudp_d = din("su_dp", (128, 64))
    wsup_d = din("ws_up", (128, 64))
    wsgt_d = din("ws_gt", (128, 64))
    wsdp_d = din("ws_dp", (1, 2048))
    consts_np = _host_consts()
    cst_d = {k: din("c_" + k, v.shape, BF16) for k, v in consts_np.items()}
    out_d = nc.dram_tensor("out", [NB, SEQ, D], F32, kind="ExternalOutput").ap()
    trace = cfg.get("trace", False)
    if trace:
        trq_d = nc.dram_tensor("tr_q", [NSC, 128, 8256], BF16, kind="ExternalOutput").ap()
        trk_d = nc.dram_tensor("tr_k", [NSC, 128, 8256], BF16, kind="ExternalOutput").ap()
        trv_d = nc.dram_tensor("tr_v", [NSC, 128, 8256], BF16, kind="ExternalOutput").ap()
        tron_d = nc.dram_tensor("tr_on", [NSC, 128, 8192], BF16, kind="ExternalOutput").ap()
        trx1_d = nc.dram_tensor("tr_x1", [NSC, 128, 8192], BF16, kind="ExternalOutput").ap()
        trb_d = nc.dram_tensor("tr_b", [NSC, 128, 64], F32, kind="ExternalOutput").ap()
        trxf_d = nc.dram_tensor("tr_x1fm", [NSC, 128, 8192], BF16, kind="ExternalOutput").ap()
        try_d = nc.dram_tensor("tr_y", [NSC, 128, 16384], BF16, kind="ExternalOutput").ap()
        trh_d = nc.dram_tensor("tr_h2", [NSC, 128, 4096], F32, kind="ExternalOutput").ap()

    expose = cfg.get("expose", ())

    def dscr(name, shape, dt=BF16):
        kind = "ExternalOutput" if name in expose else "Internal"
        return nc.dram_tensor(name, list(shape), dt, kind=kind).ap()

    wq_s = dscr("wq_s", (16, 128, 2048))
    wk_s = dscr("wk_s", (16, 128, 2048))
    wv_s = dscr("wv_s", (16, 128, 2048))
    wo_s = dscr("wo_s", (4, 16, 128, 512))
    dq_s = dscr("dq_s", (3, 128, 8192))
    upA_s = dscr("upA_s", (D, INTER))
    gtA_s = dscr("gtA_s", (D, INTER))
    dnA_s = dscr("dnA_s", (INTER, D))
    up_s = dscr("up_s", (64, 128, 2048))
    gt_s = dscr("gt_s", (64, 128, 2048))
    dn_s = dscr("dn_s", (16, 128, 8192))

    ctx = contextlib.ExitStack()
    sb = {}

    def alloc(name, shape, dt):
        t = ctx.enter_context(nc.sbuf_tensor(name, list(shape), dt))
        sb[name] = t
        return t

    for k, v in consts_np.items():
        alloc("k_" + k, v.shape, BF16)
    alloc("ccol", (128, 192), F32)
    alloc("rmsc", (128, 1), F32)
    alloc("sup", (128, 64), F32)
    alloc("sgt", (128, 64), F32)
    alloc("sucu", (128, 16), F32)
    alloc("sucd", (128, 64), F32)
    alloc("wbb", (128, 256), BF16)
    alloc("tails", (128, 192), BF16)
    alloc("st32", (128, 4096), F32)
    alloc("stbf", (128, 4096), BF16)
    alloc("beta", (128, 64), F32)
    PBF_COLS = 61888
    PF32_COLS = 8300
    alloc("pbf", (128, PBF_COLS), BF16)
    alloc("pf32", (128, PF32_COLS), F32)
    pbanks = [ctx.enter_context(nc.psum_tensor("ps%d" % i, [128, 512], F32))
              for i in range(6)]
    ptb = [ctx.enter_context(nc.psum_tensor("pt%d" % i, [128, 1024], BF16))
           for i in range(2)]

    B = {}
    for k, v in consts_np.items():
        B[k] = _Buf(kb, sb["k_" + k], 0, v.shape[1], k, "t_" + k, 2,
                    nrow=v.shape[0])
    for nm, isz in (("ccol", 4), ("rmsc", 4), ("sup", 4), ("sgt", 4),
                    ("sucu", 4), ("sucd", 4), ("wbb", 2), ("tails", 2),
                    ("beta", 4)):
        t = sb[nm]
        B[nm] = _Buf(kb, t, 0, t.shape[1], nm, "t_" + nm, isz, nrow=t.shape[0])

    ST32 = [_Buf(kb, sb["st32"], bh * 128, 128, "st32_%d" % bh, "t_st32", 4)
            for bh in range(32)]
    STBF = [_Buf(kb, sb["stbf"], bh * 128, 128, "stbf_%d" % bh, "t_stbf", 2)
            for bh in range(32)]
    STG32 = [_Buf(kb, sb["st32"], g * 512, 512, "stg32_%d" % g, "t_st32", 4)
             for g in range(8)]
    STGBF = [_Buf(kb, sb["stbf"], g * 512, 512, "stgbf_%d" % g, "t_stbf", 2)
             for g in range(8)]
    STALL32 = _Buf(kb, sb["st32"], 0, 4096, "st32_all", "t_st32", 4)
    STALLBF = _Buf(kb, sb["stbf"], 0, 4096, "stbf_all", "t_stbf", 2)

    class _PBank:
        def __init__(self, t, view):
            self.t, self.view = t, view

        def __getitem__(self, sl):
            return self.t[sl]

    PS = [_PBank(pbanks[i], kb.view(0, 1, "ps%d" % i, "zps%d" % i))
          for i in range(6)]
    PT = [_PBank(ptb[i], kb.view(0, 1, "pt%d" % i, "zpt%d" % i))
          for i in range(2)]

    dvw = {}
    for nm in ("wq_s", "wk_s", "wv_s", "wo_s", "dq_s", "upA_s", "gtA_s",
               "dnA_s", "up_s", "gt_s", "dn_s", "out"):
        dvw[nm] = kb.view(0, 1, nm, "dram_" + nm)

    def dma(q, out_ap, in_ap, reads=(), writes=()):
        kb.rec(q, lambda e, o=out_ap, i=in_ap: e.dma_start(out=o, in_=i),
               reads=reads, writes=writes, dma=True)

    def mm(out_ap, lhsT_ap, rhs_ap, reads, wbank, start, stop, inc):
        kb.rec("tensor",
               lambda e, o=out_ap, l=lhsT_ap, r=rhs_ap, s1=start, s2=stop:
               e.matmul(o, l, r, start=s1, stop=s2),
               reads=reads, writes=(wbank.view,), inc=inc)

    def tpose(out_ap, in_ap, reads, wbank, inc):
        kb.rec("tensor",
               lambda e, o=out_ap, i=in_ap: e.transpose(o, i, B["i128"].ap),
               reads=tuple(reads) + (B["i128"].view,), writes=(wbank.view,),
               inc=inc)

    def vop(fn, reads, writes):
        kb.rec("vector", fn, reads=reads, writes=writes)

    def sop(fn, reads, writes):
        kb.rec("scalar", fn, reads=reads, writes=writes)

    def bfb(col0, ncol, name, nrow=128, row0=0):
        assert col0 + ncol <= PBF_COLS, (name, col0 + ncol)
        return _Buf(kb, sb["pbf"], col0, ncol, name, "bf", 2, nrow, row0)

    def f32b(col0, ncol, name, nrow=128, row0=0):
        assert col0 + ncol <= PF32_COLS, (name, col0 + ncol)
        return _Buf(kb, sb["pf32"], col0, ncol, name, "f32", 4, nrow, row0)

    XFM = bfb(0, 8192, "xfm")
    RAWQ = bfb(8192, 8256, "rawq")
    RAWK = bfb(16448, 8256, "rawk")
    RAWV = bfb(24704, 8256, "rawv")
    KNTM = bfb(32960, 2048, "kn_tm")
    VTM = bfb(35008, 2048, "v_tm")
    XB = bfb(37056, 512, "xb")
    ANEG = bfb(37568, 512, "aneg")
    BTS = bfb(38080, 3584, "bts")
    BPP = bfb(41664, 1024, "b_pp")
    MTB = bfb(42688, 512, "mt")
    SQT = bfb(43200, 1024, "sqtmp")
    RSBF = bfb(44224, 1024, "rsbf", nrow=16)
    ONFM = bfb(45248, 8192, "onfm")
    WRING = bfb(53440, 8192, "wring")
    WORING = bfb(53440, 8192, "woring")
    DQ = bfb(0, 8192, "dq")
    YB = bfb(8192, 16384, "y")
    X1B = bfb(24704, 8192, "x1b")
    X1FM = bfb(0, 8192, "x1fm")
    UPR = bfb(32960, 4096, "upring")
    GTR = bfb(37056, 4096, "gatering")
    WDR = bfb(45248, 16384, "wdring")
    SILU = bfb(43200, 256, "silu_tmp")
    SILU2 = bfb(43456, 256, "silu_tmp2")
    SQD = bfb(8192, 2048, "sqdump")
    SDPR = bfb(41664, 2048, "sdprep")
    SAI = bfb(0, 8192, "stgA_in")
    SAO = bfb(8192, 8192, "stgA_out")
    SAI2 = bfb(16448, 8192, "stgA_in2")
    SAO2 = bfb(24704, 8192, "stgA_out2")
    WOPR = bfb(32960, 4096, "wo_prep")
    DGPR = bfb(37056, 512, "diag_prep")
    XM = f32b(0, 512, "xm")
    OTMP = f32b(512, 512, "o_tmp")
    OUTB = f32b(1024, 4096, "outbuf")
    OUTT = [_Buf(kb, sb["pf32"], 1024 + ttl * 2048, 2048, "outb%d" % ttl,
                 "f32", 4) for ttl in range(2)]
    RS32 = f32b(5120, 1024, "rs32", nrow=16)
    SUC = f32b(6144, 1, "su_col")
    STATS = f32b(6145, 16, "stats")
    RMSV = f32b(6161, 16, "rmsv")

    for nm in consts_np:
        dma("sync", B[nm].ap, cst_d[nm], writes=(B[nm].view,))
    dma("sync", B["rmsc"].ap, rms_d, writes=(B["rmsc"].view,))
    for pi in range(3):
        dma("sync", B["ccol"][:, pi * 64:(pi + 1) * 64], conv_d[pi],
            writes=(B["ccol"].view,))
    dma("gpsimd", B["wbb"].ap, wb_td.rearrange("(i p) h -> p i h", p=128),
        writes=(B["wbb"].view,))
    dma("sync", B["sup"].ap, wsup_d, writes=(B["sup"].view,))
    dma("sync", B["sgt"].ap, wsgt_d, writes=(B["sgt"].view,))
    c2048 = 1.0 / math.sqrt(2048.0)
    c8192 = 1.0 / math.sqrt(8192.0)
    vop(lambda e: e.tensor_scalar(out=B["sup"].ap, in0=B["sup"].ap,
                                  scalar1=c2048, scalar2=None, op0=ALU.mult),
        (B["sup"].view,), (B["sup"].view,))
    vop(lambda e: e.tensor_scalar(out=B["sgt"].ap, in0=B["sgt"].ap,
                                  scalar1=c2048, scalar2=None, op0=ALU.mult),
        (B["sgt"].view,), (B["sgt"].view,))
    for j in range(16):
        dma("sync", B["sucu"][:, j:j + 1], suug_d[8 * j:8 * j + 8, :],
            writes=(B["sucu"].view,))
    for j in range(64):
        dma("sync", B["sucd"][:, j:j + 1], sudp_d[2 * j:2 * j + 2, :],
            writes=(B["sucd"].view,))
    dma("gpsimd", SDPR[0:1, 0:2048], wsdp_d, writes=(SDPR.view,))
    vop(lambda e: e.tensor_scalar(out=SDPR[0:1, 0:2048], in0=SDPR[0:1, 0:2048],
                                  scalar1=c8192, scalar2=None, op0=ALU.mult),
        (SDPR.view,), (SDPR.view,))
    for ch in range(4):
        mm(PS[0][:, 0:512], B["ones1"].ap, SDPR[0:1, ch * 512:(ch + 1) * 512],
           (B["ones1"].view, SDPR.view), PS[0], True, True, True)
        vop(lambda e, ch=ch: e.tensor_copy(
            out=SDPR[:, ch * 512:(ch + 1) * 512], in_=PS[0][:, 0:512]),
            (PS[0].view,), (SDPR.view,))
    vop(lambda e: e.memset(STALL32.ap, 0.0), (), (STALL32.view,))
    vop(lambda e: e.memset(STALLBF.ap, 0.0), (), (STALLBF.view,))
    vop(lambda e: e.memset(B["tails"].ap, 0.0), (), (B["tails"].view,))

    for src, dst, nm in ((wq_td, wq_s, "wq_s"), (wk_td, wk_s, "wk_s"),
                         (wv_td, wv_s, "wv_s")):
        dma("gpsimd", dst, src, writes=(dvw[nm],))

    for oc in range(4):
        for half in range(2):
            ks = slice(half * 8, half * 8 + 8)
            dma("gpsimd", WOPR.ap, wo_td[oc, ks].rearrange("k p c -> p k c"),
                writes=(WOPR.view,))
            vop(lambda e: e.tensor_scalar(out=WOPR.ap, in0=WOPR.ap,
                                          scalar1=B["rmsc"].ap, scalar2=None,
                                          op0=ALU.mult),
                (WOPR.view, B["rmsc"].view), (WOPR.view,))
            dma("scalar", wo_s[oc, ks].rearrange("k p c -> p k c"), WOPR.ap,
                reads=(WOPR.view,), writes=(dvw["wo_s"],))

    for pi in range(3):
        for i in range(16):
            for j in range(4):
                cidx = pi * 64 + i * 4 + j
                vop(lambda e, j=j, cidx=cidx: e.tensor_scalar(
                    out=DGPR[:, j * 128:(j + 1) * 128], in0=B["i128"].ap,
                    scalar1=B["ccol"][:, cidx:cidx + 1], scalar2=None,
                    op0=ALU.mult),
                    (B["i128"].view, B["ccol"].view), (DGPR.view,))
            dma("scalar", dq_s[pi, :, i * 512:(i + 1) * 512], DGPR.ap,
                reads=(DGPR.view,), writes=(dvw["dq_s"],))

    def stage_a(src_ap, dstA, dst_view, G, M):
        for g in range(G):
            sai = SAI if g % 2 == 0 else SAI2
            sao = SAO if g % 2 == 0 else SAO2
            dma("gpsimd", sai[:, 0:M], src_ap[g * 128:(g + 1) * 128, :],
                writes=(sai.view,))
            for ch in range(M // 512):
                bank = PS[(g * (M // 512) + ch) % 4]
                mm(bank[:, 0:512], B["h128"].ap, sai[:, ch * 512:(ch + 1) * 512],
                   (B["h128"].view, sai.view), bank, True, True, True)
                sop(lambda e, ch=ch, bank=bank, sao=sao: e.copy(
                    out=sao[:, ch * 512:(ch + 1) * 512], in_=bank[:, 0:512]),
                    (bank.view,), (sao.view,))
            dma("scalar", dstA[g * 128:(g + 1) * 128, :], sao[:, 0:M],
                reads=(sao.view,), writes=(dst_view,))

    stage_a(wup_d, upA_s, dvw["upA_s"], 16, INTER)
    stage_a(wg_d, gtA_s, dvw["gtA_s"], 16, INTER)
    stage_a(wd_d, dnA_s, dvw["dnA_s"], 64, D)

    def stage_b_ug(srcA, srcA_view, dst, dst_view):
        for j in range(16):
            sai = SAI if j % 2 == 0 else SAI2
            sao = SAO if j % 2 == 0 else SAO2
            src = srcA.rearrange("(g w) c -> g w c", w=128)[:, 8 * j:8 * j + 8, :]
            src = src.rearrange("g w c -> w g c")
            dma("gpsimd", sai[:, 0:INTER], src, reads=(srcA_view,),
                writes=(sai.view,))
            for ch in range(INTER // 512):
                bank = PS[(j * 16 + ch) % 4]
                mm(bank[:, 0:512], B["bd16"].ap, sai[:, ch * 512:(ch + 1) * 512],
                   (B["bd16"].view, sai.view), bank, True, True, True)
                vop(lambda e, ch=ch, bank=bank, j=j, sao=sao: e.tensor_scalar(
                    out=sao[:, ch * 512:(ch + 1) * 512], in0=bank[:, 0:512],
                    scalar1=B["sucu"][:, j:j + 1], scalar2=None, op0=ALU.mult),
                    (bank.view, B["sucu"].view), (sao.view,))
            for vg in range(8):
                dd = dst.rearrange("v w (g c) -> v w g c", c=128)[
                    vg * 8:vg * 8 + 8, 8 * j:8 * j + 8, :, :]
                dd = dd.rearrange("v w g c -> w g v c")
                srcb = sao[:, vg * 1024:(vg + 1) * 1024].rearrange(
                    "p (v c) -> p v c", c=128)
                dma("scalar", dd, srcb, reads=(sao.view,), writes=(dst_view,))

    stage_b_ug(upA_s, dvw["upA_s"], up_s, dvw["up_s"])
    stage_b_ug(gtA_s, dvw["gtA_s"], gt_s, dvw["gt_s"])

    for j in range(64):
        sai = SAI if j % 2 == 0 else SAI2
        sao = SAO if j % 2 == 0 else SAO2
        src = dnA_s.rearrange("(g w) c -> g w c", w=128)[:, 2 * j:2 * j + 2, :]
        src = src.rearrange("g w c -> w g c")
        dma("gpsimd", sai[:, 0:D], src, reads=(dvw["dnA_s"],),
            writes=(sai.view,))
        for ch in range(4):
            bank = PS[(j * 4 + ch) % 4]
            mm(bank[:, 0:512], B["bd64"].ap, sai[:, ch * 512:(ch + 1) * 512],
               (B["bd64"].view, sai.view), bank, True, True, True)
            vop(lambda e, ch=ch, bank=bank, j=j, sao=sao: e.scalar_tensor_tensor(
                out=sao[:, ch * 512:(ch + 1) * 512], in0=bank[:, 0:512],
                scalar=B["sucd"][:, j:j + 1],
                in1=SDPR[:, ch * 512:(ch + 1) * 512],
                op0=ALU.mult, op1=ALU.mult),
                (bank.view, B["sucd"].view, SDPR.view), (sao.view,))
        for q4 in range(4):
            dd = dn_s.rearrange("o p (v c) -> o p v c", c=128)[
                q4 * 4:q4 * 4 + 4, 2 * j:2 * j + 2, :, :]
            dd = dd.rearrange("o p v c -> p v o c")
            srcb = sao[:, q4 * 512:(q4 + 1) * 512].rearrange(
                "p (o c) -> p o c", c=128)
            dma("scalar", dd, srcb, reads=(sao.view,), writes=(dvw["dn_s"],))

    RAW = {0: RAWQ, 1: RAWK, 2: RAWV}
    W_S = {0: wq_s, 1: wk_s, 2: wv_s}
    WSV = {0: dvw["wq_s"], 1: dvw["wk_s"], 2: dvw["wv_s"]}

    def layernorm(srcbuf, dst_ap_fn, out_bf16):
        vop(lambda e: e.memset(STATS[:, 0:2], 0.0), (), (STATS.view,))
        sop(lambda e: e.activation(out=SQD.ap, in_=srcbuf.ap, func=AF.Copy,
                                   accum_out=STATS[:, 0:1]),
            (srcbuf.view,), (SQD.view, STATS.view))
        sop(lambda e: e.activation(out=SQD.ap, in_=srcbuf.ap, func=AF.Square,
                                   accum_out=STATS[:, 1:2]),
            (srcbuf.view,), (SQD.view, STATS.view))
        vop(lambda e: e.tensor_scalar(out=STATS[:, 2:3], in0=STATS[:, 0:1],
                                      scalar1=1.0 / 2048.0, scalar2=None,
                                      op0=ALU.mult),
            (STATS.view,), (STATS.view,))
        vop(lambda e: e.tensor_scalar(out=STATS[:, 3:4], in0=STATS[:, 1:2],
                                      scalar1=1.0 / 2048.0, scalar2=LN_EPS,
                                      op0=ALU.mult, op1=ALU.add),
            (STATS.view,), (STATS.view,))
        vop(lambda e: e.tensor_tensor(out=STATS[:, 4:5], in0=STATS[:, 2:3],
                                      in1=STATS[:, 2:3], op=ALU.mult),
            (STATS.view,), (STATS.view,))
        vop(lambda e: e.tensor_tensor(out=STATS[:, 5:6], in0=STATS[:, 3:4],
                                      in1=STATS[:, 4:5], op=ALU.subtract),
            (STATS.view,), (STATS.view,))
        vop(lambda e: e.reciprocal(out=STATS[:, 6:7], in_=STATS[:, 5:6]),
            (STATS.view,), (STATS.view,))
        sop(lambda e: e.activation(out=STATS[:, 7:8], in_=STATS[:, 6:7],
                                   func=AF.Sqrt),
            (STATS.view,), (STATS.view,))
        dst_ap, dst_views = dst_ap_fn()
        vop(lambda e, dst_ap=dst_ap: e.tensor_scalar(
            out=dst_ap, in0=srcbuf.ap, scalar1=STATS[:, 2:3],
            scalar2=STATS[:, 7:8], op0=ALU.subtract, op1=ALU.mult),
            (srcbuf.view, STATS.view), dst_views)

    for s in range(n_sc):
        b = s // 4
        sc = s % 4
        t0 = sc * SCT

        src = x_fm_d[b].rearrange("(i p) t -> p i t", p=128)[:, :, t0:t0 + SCT]
        dma("gpsimd", XFM.ap, src, writes=(XFM.view,))
        if sc == 0 and s > 0:
            vop(lambda e: e.memset(STALL32.ap, 0.0), (), (STALL32.view,))
            vop(lambda e: e.memset(STALLBF.ap, 0.0), (), (STALLBF.view,))

        for pi in range(3):
            raw = RAW[pi]
            for o in range(16):
                slot = o % 4
                dma("sync", WRING[:, slot * 2048:(slot + 1) * 2048],
                    W_S[pi][o], reads=(WSV[pi],), writes=(WRING.view,))
                bank = PS[o % 4]
                for i in range(16):
                    mm(bank[:, 0:512],
                       WRING[:, slot * 2048 + i * 128:slot * 2048 + (i + 1) * 128],
                       XFM[:, i * 512:(i + 1) * 512],
                       (WRING.view, XFM.view), bank, i == 0, i == 15, i == 15)
                sop(lambda e, raw=raw, o=o, bank=bank: e.copy(
                    out=raw[:, o * 516 + 4:o * 516 + 516], in_=bank[:, 0:512]),
                    (bank.view,), (raw.view,))
                tcol = pi * 64 + o * 4
                if sc == 0:
                    vop(lambda e, raw=raw, o=o: e.memset(
                        raw[:, o * 516:o * 516 + 4], 0.0), (), (raw.view,))
                else:
                    vop(lambda e, raw=raw, o=o, tcol=tcol: e.tensor_copy(
                        out=raw[:, o * 516:o * 516 + 4],
                        in_=B["tails"][:, tcol:tcol + 4]),
                        (B["tails"].view,), (raw.view,))
                vop(lambda e, raw=raw, o=o, tcol=tcol: e.tensor_copy(
                    out=B["tails"][:, tcol:tcol + 4],
                    in_=raw[:, o * 516 + 512:o * 516 + 516]),
                    (raw.view,), (B["tails"].view,))

        if attn_stage < 2:
            continue
        for tt in range(4):
            for i in range(16):
                mm(PS[4][:, tt * 16:(tt + 1) * 16],
                   XFM[:, i * 512 + tt * 128:i * 512 + (tt + 1) * 128],
                   B["wbb"][:, i * 16:(i + 1) * 16],
                   (XFM.view, B["wbb"].view), PS[4], i == 0, i == 15, i == 15)
        sop(lambda e: e.activation(out=B["beta"].ap, in_=PS[4][:, 0:64],
                                   func=AF.Sigmoid),
            (PS[4].view,), (B["beta"].view,))

        if attn_stage < 3:
            continue
        for pi in range(3):
            raw = RAW[pi]
            dma("sync", DQ.ap, dq_s[pi], reads=(dvw["dq_s"],), writes=(DQ.view,))
            for o in range(16):
                bank = PS[4 + o % 2]
                for j in range(4):
                    mm(bank[:, 0:512],
                       DQ[:, o * 512 + j * 128:o * 512 + (j + 1) * 128],
                       raw[:, o * 516 + 1 + j:o * 516 + 513 + j],
                       (DQ.view, raw.view), bank, j == 0, j == 3, j == 3)
                sop(lambda e, raw=raw, o=o, bank=bank: e.activation(
                    out=raw[:, o * 516 + 4:o * 516 + 516], in_=bank[:, 0:512],
                    func=AF.Silu),
                    (bank.view,), (raw.view,))

        if attn_stage < 4:
            continue
        for pi, nscale in ((0, 1.0 / 128.0), (1, 1.0)):
            raw = RAW[pi]
            half = pi * 512
            for h in range(16):
                sl = SQT[:, (h % 2) * 512:(h % 2) * 512 + 512]
                sop(lambda e, raw=raw, h=h, sl=sl: e.activation(
                    out=sl, in_=raw[:, h * 516 + 4:h * 516 + 516],
                    func=AF.Square),
                    (raw.view,), (SQT.view,))
                mm(PS[4][0:16, 0:512], B["esum"][:, 16 * h:16 * h + 16], sl,
                   (B["esum"].view, SQT.view), PS[4], h == 0, h == 15, True)
            if attn_stage < 41:
                continue
            vop(lambda e, half=half: e.tensor_scalar(
                out=RS32[:, half:half + 512], in0=PS[4][0:16, 0:512],
                scalar1=L2_EPS, scalar2=None, op0=ALU.add),
                (PS[4].view,), (RS32.view,))
            vop(lambda e, half=half: e.reciprocal(
                out=RS32[:, half:half + 512], in_=RS32[:, half:half + 512]),
                (RS32.view,), (RS32.view,))
            if attn_stage < 42:
                continue
            sop(lambda e, half=half, nscale=nscale: e.activation(
                out=RSBF[:, half:half + 512], in_=RS32[:, half:half + 512],
                func=AF.Sqrt, scale=nscale),
                (RS32.view,), (RSBF.view,))
            if attn_stage < 43:
                continue
            for h in range(16):
                mm(PS[5][:, 0:512], B["ebc"][:, 128 * h:128 * h + 128],
                   RSBF[:, half:half + 512],
                   (B["ebc"].view, RSBF.view), PS[5], True, True, True)
                if attn_stage < 44:
                    continue
                vop(lambda e, raw=raw, h=h: e.tensor_tensor(
                    out=raw[:, h * 516 + 4:h * 516 + 516],
                    in0=raw[:, h * 516 + 4:h * 516 + 516],
                    in1=PS[5][:, 0:512], op=ALU.mult),
                    (raw.view, PS[5].view), (raw.view,))

        if attn_stage < 50:
            continue
        for c in range(dlim[0]):
            cc = 4 + c * 128
            bcol = c * 16
            for hb in range(dlim[1]):
                hs = [4 * hb + k for k in range(4)]
                bhs = [b * 16 + h for h in hs]
                g = b * 4 + hb
                for k, h in enumerate(hs):
                    tpose(PT[0][:, k * 128:(k + 1) * 128],
                          RAWK[:, h * 516 + cc:h * 516 + cc + 128],
                          (RAWK.view,), PT[0], k == 3)
                vop(lambda e, hb=hb: e.tensor_copy(
                    out=KNTM[:, hb * 512:(hb + 1) * 512], in_=PT[0][:, 0:512]),
                    (PT[0].view,), (KNTM.view,))
                for k, h in enumerate(hs):
                    tpose(PT[0][:, 512 + k * 128:512 + (k + 1) * 128],
                          RAWV[:, h * 516 + cc:h * 516 + cc + 128],
                          (RAWV.view,), PT[0], k == 3)
                vop(lambda e, hb=hb: e.tensor_copy(
                    out=VTM[:, hb * 512:(hb + 1) * 512],
                    in_=PT[0][:, 512:1024]),
                    (PT[0].view,), (VTM.view,))
                for k, h in enumerate(hs):
                    mm(PS[3][:, k * 128:(k + 1) * 128],
                       RAWK[:, h * 516 + cc:h * 516 + cc + 128],
                       RAWK[:, h * 516 + cc:h * 516 + cc + 128],
                       (RAWK.view,), PS[3], True, True, k == 3)
                for k, h in enumerate(hs):
                    sop(lambda e, k=k, h=h, bcol=bcol: e.activation(
                        out=ANEG[:, k * 128:(k + 1) * 128],
                        in_=PS[3][:, k * 128:(k + 1) * 128], func=AF.Copy,
                        scale=B["beta"][:, bcol + h:bcol + h + 1]),
                        (PS[3].view, B["beta"].view), (ANEG.view,))
                vop(lambda e: e.tensor_tensor(out=ANEG.ap, in0=ANEG.ap,
                                              in1=B["mstneg"].ap, op=ALU.mult),
                    (ANEG.view, B["mstneg"].view), (ANEG.view,))
                for k in range(4):
                    tpose(PT[1][:, k * 128:(k + 1) * 128],
                          ANEG[:, k * 128:(k + 1) * 128],
                          (ANEG.view,), PT[1], k == 3)
                vop(lambda e: e.tensor_copy(out=BTS[:, 0:512],
                                            in_=PT[1][:, 0:512]),
                    (PT[1].view,), (BTS.view,))
                for lev in range(1, 7):
                    bt0 = (lev - 1) * 512

                    def bprev_ap(k, lev=lev):
                        if lev == 1:
                            return ANEG[:, k * 128:(k + 1) * 128]
                        off = ((lev - 1) % 2) * 512
                        return BPP[:, off + k * 128:off + (k + 1) * 128]

                    bprev_view = ANEG.view if lev == 1 else BPP.view
                    if lev < 6:
                        for k in range(4):
                            mm(PS[3][:, k * 128:(k + 1) * 128],
                               BTS[:, bt0 + k * 128:bt0 + (k + 1) * 128],
                               bprev_ap(k),
                               (BTS.view, bprev_view), PS[3], True, True,
                               k == 3)
                        off = (lev % 2) * 512
                        vop(lambda e, off=off: e.tensor_copy(
                            out=BPP[:, off:off + 512], in_=PS[3][:, 0:512]),
                            (PS[3].view,), (BPP.view,))
                    for k in range(4):
                        mm(PS[4][:, k * 128:(k + 1) * 128],
                           bprev_ap(k),
                           BTS[:, bt0 + k * 128:bt0 + (k + 1) * 128],
                           (BTS.view, bprev_view), PS[4], True, True, k == 3)
                    vop(lambda e, lev=lev: e.tensor_copy(
                        out=BTS[:, lev * 512:(lev + 1) * 512],
                        in_=PS[4][:, 0:512]),
                        (PS[4].view,), (BTS.view,))
                for k, (h, bh) in enumerate(zip(hs, bhs)):
                    mm(PS[5][:, k * 128:(k + 1) * 128],
                       RAWK[:, h * 516 + cc:h * 516 + cc + 128],
                       STBF[bh].ap,
                       (RAWK.view, STBF[bh].view), PS[5], True, True, k == 3)
                vop(lambda e, hb=hb: e.tensor_tensor(
                    out=XM.ap, in0=VTM[:, hb * 512:(hb + 1) * 512],
                    in1=PS[5][:, 0:512], op=ALU.subtract),
                    (VTM.view, PS[5].view), (XM.view,))
                for k, h in enumerate(hs):
                    vop(lambda e, k=k, h=h, bcol=bcol: e.tensor_scalar(
                        out=XM[:, k * 128:(k + 1) * 128],
                        in0=XM[:, k * 128:(k + 1) * 128],
                        scalar1=B["beta"][:, bcol + h:bcol + h + 1],
                        scalar2=None, op0=ALU.mult),
                        (XM.view, B["beta"].view), (XM.view,))
                vop(lambda e: e.tensor_copy(out=XB.ap, in_=XM.ap),
                    (XM.view,), (XB.view,))
                for lev in range(6, -1, -1):
                    for k in range(4):
                        mm(PS[5][:, k * 128:(k + 1) * 128],
                           BTS[:, lev * 512 + k * 128:lev * 512 + (k + 1) * 128],
                           XB[:, k * 128:(k + 1) * 128],
                           (BTS.view, XB.view), PS[5], True, True, k == 3)
                    vop(lambda e: e.tensor_tensor(out=XM.ap, in0=XM.ap,
                                                  in1=PS[5][:, 0:512],
                                                  op=ALU.add),
                        (XM.view, PS[5].view), (XM.view,))
                    vop(lambda e: e.tensor_copy(out=XB.ap, in_=XM.ap),
                        (XM.view,), (XB.view,))
                for k, h in enumerate(hs):
                    mm(PS[3][:, k * 128:(k + 1) * 128],
                       RAWK[:, h * 516 + cc:h * 516 + cc + 128],
                       RAWQ[:, h * 516 + cc:h * 516 + cc + 128],
                       (RAWK.view, RAWQ.view), PS[3], True, True, k == 3)
                vop(lambda e: e.tensor_tensor(out=MTB.ap, in0=PS[3][:, 0:512],
                                              in1=B["mtriu"].ap, op=ALU.mult),
                    (PS[3].view, B["mtriu"].view), (MTB.view,))
                for k, (h, bh) in enumerate(zip(hs, bhs)):
                    mm(PS[2][:, k * 128:(k + 1) * 128],
                       RAWQ[:, h * 516 + cc:h * 516 + cc + 128],
                       STBF[bh].ap,
                       (RAWQ.view, STBF[bh].view), PS[2], True, False, False)
                    mm(PS[2][:, k * 128:(k + 1) * 128],
                       MTB[:, k * 128:(k + 1) * 128],
                       XB[:, k * 128:(k + 1) * 128],
                       (MTB.view, XB.view), PS[2], False, True, k == 3)
                vop(lambda e: e.tensor_copy(out=OTMP.ap, in_=PS[2][:, 0:512]),
                    (PS[2].view,), (OTMP.view,))
                for k, (h, bh) in enumerate(zip(hs, bhs)):
                    mm(PS[5][:, k * 128:(k + 1) * 128],
                       KNTM[:, hb * 512 + k * 128:hb * 512 + (k + 1) * 128],
                       XB[:, k * 128:(k + 1) * 128],
                       (KNTM.view, XB.view), PS[5], True, True, k == 3)
                vop(lambda e, g=g: e.tensor_tensor(
                    out=STG32[g].ap, in0=STG32[g].ap, in1=PS[5][:, 0:512],
                    op=ALU.add),
                    (STG32[g].view, PS[5].view), (STG32[g].view,))
                vop(lambda e, g=g: e.tensor_copy(out=STGBF[g].ap,
                                                 in_=STG32[g].ap),
                    (STG32[g].view,), (STGBF[g].view,))
                vop(lambda e: e.tensor_tensor(out=XM.ap, in0=OTMP.ap,
                                              in1=OTMP.ap, op=ALU.mult),
                    (OTMP.view,), (XM.view,))
                vop(lambda e: e.reduce_sum(
                    out=RMSV[:, 0:4],
                    in_=XM.ap.rearrange("p (s c) -> p s c", c=128),
                    axis=mybir.AxisListType.X),
                    (XM.view,), (RMSV.view,))
                vop(lambda e: e.tensor_scalar(
                    out=RMSV[:, 0:4], in0=RMSV[:, 0:4], scalar1=1.0 / 128.0,
                    scalar2=RMS_EPS, op0=ALU.mult, op1=ALU.add),
                    (RMSV.view,), (RMSV.view,))
                vop(lambda e: e.reciprocal(out=RMSV[:, 4:8], in_=RMSV[:, 0:4]),
                    (RMSV.view,), (RMSV.view,))
                sop(lambda e: e.activation(out=RMSV[:, 8:12], in_=RMSV[:, 4:8],
                                           func=AF.Sqrt),
                    (RMSV.view,), (RMSV.view,))
                for k in range(4):
                    vop(lambda e, k=k: e.tensor_scalar(
                        out=XB[:, k * 128:(k + 1) * 128],
                        in0=OTMP[:, k * 128:(k + 1) * 128],
                        scalar1=RMSV[:, 8 + k:9 + k], scalar2=None,
                        op0=ALU.mult),
                        (OTMP.view, RMSV.view), (XB.view,))
                for k in range(4):
                    tpose(PT[1][:, k * 128:(k + 1) * 128],
                          XB[:, k * 128:(k + 1) * 128],
                          (XB.view,), PT[1], k == 3)
                vop(lambda e, hb=hb, c=c: e.tensor_copy(
                    out=ONFM.ap.rearrange("p (h t) -> p h t", t=512)[
                        :, 4 * hb:4 * hb + 4, c * 128:(c + 1) * 128],
                    in_=PT[1][:, 0:512].rearrange("p (h t) -> p h t", t=128)),
                    (PT[1].view,), (ONFM.view,))

        if trace:
            dma("scalar", trq_d[s], RAWQ.ap, reads=(RAWQ.view,), writes=(dvw["out"],))
            dma("scalar", trk_d[s], RAWK.ap, reads=(RAWK.view,), writes=(dvw["out"],))
            dma("scalar", trv_d[s], RAWV.ap, reads=(RAWV.view,), writes=(dvw["out"],))
            dma("scalar", tron_d[s], ONFM.ap, reads=(ONFM.view,), writes=(dvw["out"],))
            dma("scalar", trb_d[s], B["beta"].ap, reads=(B["beta"].view,), writes=(dvw["out"],))
        if attn_stage < 60:
            continue
        for pair in range(2):
            tts = (2 * pair, 2 * pair + 1)
            for ttl, tt in enumerate(tts):
                dma("scalar", OUTT[ttl].ap,
                    x_tm_d[b, t0 + tt * 128:t0 + (tt + 1) * 128, :],
                    writes=(OUTT[ttl].view,))
            for oc in range(4):
                for half in range(2):
                    dma("sync", WORING[:, half * 4096:(half + 1) * 4096],
                        wo_s[oc, half * 8:half * 8 + 8]
                        .rearrange("k p c -> p k c"),
                        reads=(dvw["wo_s"],), writes=(WORING.view,))
                for ttl, tt in enumerate(tts):
                    bank = PS[(oc * 2 + ttl) % 4]
                    for kk in range(16):
                        woff = (kk // 8) * 4096 + (kk % 8) * 512
                        mm(bank[:, 0:512],
                           ONFM[:, kk * 512 + tt * 128:kk * 512 + (tt + 1) * 128],
                           WORING[:, woff:woff + 512],
                           (ONFM.view, WORING.view), bank, kk == 0, kk == 15,
                           kk == 15)
                    vop(lambda e, ttl=ttl, oc=oc, bank=bank: e.tensor_tensor(
                        out=OUTT[ttl][:, oc * 512:(oc + 1) * 512],
                        in0=OUTT[ttl][:, oc * 512:(oc + 1) * 512],
                        in1=bank[:, 0:512], op=ALU.add),
                        (OUTT[ttl].view, bank.view), (OUTT[ttl].view,))
            for ttl, tt in enumerate(tts):
                layernorm(OUTT[ttl],
                          lambda tt=tt: (X1B[:, tt * 2048:(tt + 1) * 2048],
                                         (X1B.view,)),
                          True)
                for ig in range(4):
                    for k in range(4):
                        i = ig * 4 + k
                        tpose(PT[0][:, k * 128:(k + 1) * 128],
                              X1B[:, tt * 2048 + i * 128:tt * 2048 + (i + 1) * 128],
                              (X1B.view,), PT[0], k == 3)
                    vop(lambda e, ig=ig, tt=tt: e.tensor_copy(
                        out=X1FM.ap.rearrange("p (i t) -> p i t", t=512)[
                            :, ig * 4:ig * 4 + 4, tt * 128:(tt + 1) * 128],
                        in_=PT[0][:, 0:512].rearrange("p (i t) -> p i t",
                                                      t=128)),
                        (PT[0].view,), (X1FM.view,))

        if trace:
            dma("scalar", trx1_d[s], X1B.ap, reads=(X1B.view,), writes=(dvw["out"],))
            dma("scalar", trxf_d[s], X1FM.ap, reads=(X1FM.view,), writes=(dvw["out"],))
        if skip_ffn:
            continue
        for tc in range(2):
            for v in range(64):
                slot = v % 2
                dma("sync", UPR[:, slot * 2048:(slot + 1) * 2048], up_s[v],
                    reads=(dvw["up_s"],), writes=(UPR.view,))
                dma("sync", GTR[:, slot * 2048:(slot + 1) * 2048], gt_s[v],
                    reads=(dvw["gt_s"],), writes=(GTR.view,))
                ub = PS[(v % 2) * 2]
                gb = PS[(v % 2) * 2 + 1]
                for i in range(16):
                    mm(ub[:, 0:256],
                       UPR[:, slot * 2048 + i * 128:slot * 2048 + (i + 1) * 128],
                       X1FM[:, i * 512 + tc * 256:i * 512 + (tc + 1) * 256],
                       (UPR.view, X1FM.view), ub, i == 0, i == 15, i == 15)
                for i in range(16):
                    mm(gb[:, 0:256],
                       GTR[:, slot * 2048 + i * 128:slot * 2048 + (i + 1) * 128],
                       X1FM[:, i * 512 + tc * 256:i * 512 + (tc + 1) * 256],
                       (GTR.view, X1FM.view), gb, i == 0, i == 15, i == 15)
                sil = SILU if v % 2 == 0 else SILU2
                sop(lambda e, v=v, gb=gb, sil=sil: e.activation(
                    out=sil.ap, in_=gb[:, 0:256], func=AF.Silu,
                    scale=B["sgt"][:, v:v + 1]),
                    (gb.view, B["sgt"].view), (sil.view,))
                vop(lambda e, v=v, ub=ub, sil=sil: e.scalar_tensor_tensor(
                    out=YB[:, v * 256:(v + 1) * 256], in0=ub[:, 0:256],
                    scalar=B["sup"][:, v:v + 1], in1=sil.ap,
                    op0=ALU.mult, op1=ALU.mult),
                    (ub.view, B["sup"].view, sil.view), (YB.view,))
            if trace and tc == 0:
                dma("scalar", try_d[s], YB.ap, reads=(YB.view,), writes=(dvw["out"],))
            for ocq in range(16):
                slot = ocq % 2
                dma("sync", WDR[:, slot * 8192:(slot + 1) * 8192], dn_s[ocq],
                    reads=(dvw["dn_s"],), writes=(WDR.view,))
                for ttl in range(2):
                    tt = tc * 2 + ttl
                    bank = PS[2 + (ocq * 2 + ttl) % 4]
                    for v in range(64):
                        mm(bank[:, 0:128],
                           YB[:, v * 256 + ttl * 128:v * 256 + (ttl + 1) * 128],
                           WDR[:, slot * 8192 + v * 128:slot * 8192 + (v + 1) * 128],
                           (YB.view, WDR.view), bank, v == 0, v == 63, v == 63)
                    vop(lambda e, ttl=ttl, tt=tt, ocq=ocq, bank=bank:
                        e.scalar_tensor_tensor(
                            out=OUTT[ttl][:, ocq * 128:(ocq + 1) * 128],
                            in0=X1B[:, tt * 2048 + ocq * 128:
                                    tt * 2048 + (ocq + 1) * 128],
                            scalar=1.0, in1=bank[:, 0:128],
                            op0=ALU.mult, op1=ALU.add),
                        (X1B.view, bank.view), (OUTT[ttl].view,))
            if trace and tc == 0:
                dma("scalar", trh_d[s], sb["pf32"][0:128, 1024:5120],
                    reads=(OUTT[0].view, OUTT[1].view), writes=(dvw["out"],))
            for ttl in range(2):
                tt = tc * 2 + ttl
                layernorm(OUTT[ttl],
                          lambda ttl=ttl: (OUTT[ttl].ap, (OUTT[ttl].view,)),
                          False)
                dma("scalar", out_d[b, t0 + tt * 128:t0 + (tt + 1) * 128, :],
                    OUTT[ttl].ap, reads=(OUTT[ttl].view,),
                    writes=(dvw["out"],))

    for (oname, pool, col0, ncol, nrow) in dumps:
        isz = 2 if pool == "pbf" else 4
        dt = BF16 if pool == "pbf" else F32
        od = nc.dram_tensor(oname, [nrow, ncol], dt,
                            kind="ExternalOutput").ap()
        dbuf = _Buf(kb, sb[pool], col0, ncol, "dump_" + oname,
                    "bf" if pool == "pbf" else "f32", isz, nrow=nrow)
        dma("scalar", od, dbuf.ap, reads=(dbuf.view,), writes=(dvw["out"],))

    sems = {}
    for sname in kb.counts:
        sems[sname] = ctx.enter_context(nc.semaphore(sname))
    for sname in kb.counts:
        nc.gpsimd.sem_clear(sems[sname])
    nc.all_engine_barrier()
    with nc.Block() as block:
        _emit(kb, nc, block, sems)
    ctx.close()
    return nc


def _emit(kb, nc, block, sems):
    for eng_name in ("sync", "gpsimd", "vector", "scalar", "tensor"):
        ops = kb.ops[eng_name]
        if not ops:
            continue

        def section(eng, ops=ops, sems=sems):
            for waits, fn, stream, step in ops:
                for sname, v in waits.items():
                    eng.wait_ge(sems[sname], v)
                inst = fn(eng)
                if stream is not None:
                    inst.then_inc(sems[stream], step)

        getattr(block, eng_name)(section)

    def fence(eng, sems=sems, counts=dict(kb.counts)):
        for sname, cval in counts.items():
            if cval:
                eng.wait_ge(sems[sname], cval)

    block.sync(fence)


def _host_weights(inputs):
    w = {}
    Wq = np.asarray(inputs["L0_attn_Wq"], dtype=np.float32)
    Wk = np.asarray(inputs["L0_attn_Wk"], dtype=np.float32)
    Wv = np.asarray(inputs["L0_attn_Wv"], dtype=np.float32)
    Wo = np.asarray(inputs["L0_attn_Wo"], dtype=np.float32)
    for nm, W in (("wq_t", Wq), ("wk_t", Wk), ("wv_t", Wv)):
        a = W.reshape(16, 128, 16, 128).transpose(0, 3, 2, 1)
        w[nm] = np.ascontiguousarray(a.reshape(16, 128, 2048))
    a = Wo.reshape(4, 512, 16, 128).transpose(0, 2, 3, 1)
    w["wo_t"] = np.ascontiguousarray(a)
    w["wb_t"] = np.ascontiguousarray(
        np.asarray(inputs["L0_attn_Wb"], dtype=np.float32).T)
    cw = []
    for nm in ("conv_q", "conv_k", "conv_v"):
        cc = np.asarray(inputs["L0_attn_" + nm], dtype=np.float32)
        cw.append(cc.reshape(16, 128, 4).transpose(1, 0, 2).reshape(128, 64))
    w["conv_w"] = np.ascontiguousarray(np.stack(cw))
    w["rms_w"] = np.ascontiguousarray(
        np.asarray(inputs["L0_attn_rms_w"], dtype=np.float32).reshape(128, 1))
    w["wup_t"] = np.ascontiguousarray(
        np.asarray(inputs["L0_ffn_W_up"], dtype=np.float32).T)
    w["wg_t"] = np.ascontiguousarray(
        np.asarray(inputs["L0_ffn_W_gate"], dtype=np.float32).T)
    w["wd_t"] = np.ascontiguousarray(
        np.asarray(inputs["L0_ffn_W_down"], dtype=np.float32).T)
    w["su_ug"] = np.ascontiguousarray(
        np.asarray(inputs["L0_ffn_SU_ug"], dtype=np.float32).reshape(16, 128).T)
    w["su_dp"] = np.ascontiguousarray(
        np.asarray(inputs["L0_ffn_SU_dp"], dtype=np.float32).reshape(64, 128).T)
    ws = np.asarray(inputs["L0_ffn_Wscale_ug"], dtype=np.float32)
    w["ws_up"] = np.ascontiguousarray(ws[:INTER].reshape(64, 128).T)
    w["ws_gt"] = np.ascontiguousarray(ws[INTER:].reshape(64, 128).T)
    w["ws_dp"] = np.ascontiguousarray(
        np.asarray(inputs["L0_ffn_Wscale_dp"],
                   dtype=np.float32).reshape(1, 2048))
    for k, v in _host_consts().items():
        w["c_" + k] = v
    return w


def kernel(**inputs):
    nc = build_program()
    w = _host_weights(inputs)
    x = np.asarray(inputs["L0_attn_x"], dtype=np.float32)
    in_maps = []
    for core in range(NCORES):
        xs = np.ascontiguousarray(x[2 * core:2 * core + 2])
        m = dict(w)
        m["x_tm"] = xs
        m["x_fm"] = np.ascontiguousarray(xs.transpose(0, 2, 1))
        in_maps.append(m)
    res = run_bass_kernel_spmd(nc, in_maps, core_ids=list(range(NCORES)))
    return np.concatenate([r["out"] for r in res.results], axis=0)
